# version 33
# baseline (speedup 1.0000x reference)
import numpy as np
import contextlib
import math
import concourse.bass as bass
import concourse.mybir as mybir
from concourse.bass_utils import run_bass_kernel_spmd

F32 = mybir.dt.float32
BF16 = mybir.dt.bfloat16
I32 = mybir.dt.int32
AF = mybir.ActivationFunctionType
ALU = mybir.AluOpType
f32 = np.float32

S = 4096
H = 2048
O = 5504
OC = O // 128
HC = H // 128
NH = 16
ST = S // 128
TWO_PI = float(2 * np.pi)
C1 = float(np.float32(2 * np.pi))
C2 = float(np.float64(2 * np.pi) - np.float64(np.float32(2 * np.pi)))
LN1E4 = float(np.log(10000.0))
EXP_SCALE = 1.0 / math.sqrt(128.0)
EXP_BIAS = -8.0
NEG_MASK = -30000.0

DEBUG_TAPS = ()
P6_SERIAL = False
P1_ILV = True
LN1_ENG = "vector"
P6_NQC = 8

ENGS = ("sync", "scalar", "vector", "tensor", "gpsimd")
ALL_PHASES = ("p0", "p1", "p2", "p3", "p4", "p5", "p6", "p7", "p8")


def build_program(phases=ALL_PHASES):
    nc = bass.Bass("TRN2", target_bir_lowering=False)

    x_in = nc.dram_tensor("x", [S, H], F32, kind="ExternalInput").ap()
    wqs_in = nc.dram_tensor("wqs", [256, H], F32, kind="ExternalInput").ap()
    wks_in = nc.dram_tensor("wks", [256, H], F32, kind="ExternalInput").ap()
    wvs_in = nc.dram_tensor("wvs", [256, H], F32, kind="ExternalInput").ap()
    wos_in = nc.dram_tensor("wos", [256, H], F32, kind="ExternalInput").ap()
    lut_in = nc.dram_tensor("lut", [1, 16], F32, kind="ExternalInput").ap()
    guidxT_in = nc.dram_tensor("guidxT", [512, O], I32,
                               kind="ExternalInput").ap()
    didxT_in = nc.dram_tensor("didxTs", [768, H], I32,
                              kind="ExternalInput").ap()
    y_out = nc.dram_tensor("y", [S, H], F32, kind="ExternalOutput").ap()

    def scr(name, shape, dtype):
        kind = "ExternalOutput" if name in DEBUG_TAPS else "Internal"
        return nc.dram_tensor(name, shape, dtype, kind=kind).ap()

    PIECES = [(0, 256), (256, 256), (512, 512), (1024, 1024),
              (2048, 1024), (3072, 1024), (4096, 1024), (5120, 384)]
    NGU = len(PIECES)
    g_sh = [scr(f"g_sh{b}", [512, w], BF16) for b, (c0, w) in
            enumerate(PIECES)]
    g_gup = [nc.dram_tensor(f"g_gup{b}", [4096, w], BF16,
                            kind="Internal", addr_space="Shared").ap()
             for b, (c0, w) in enumerate(PIECES)]
    g_shd = scr("g_shd", [768, H], BF16)
    g_wdp = nc.dram_tensor("g_wdp", [6144, H], BF16,
                           kind="Internal", addr_space="Shared").ap()
    g_bar_i = scr("g_bar_i", [1, 64], I32)
    g_bar_o = scr("g_bar_o", [1, 64], I32)
    g_h = scr("g_h", [OC * 128, S], BF16)
    g_y1 = scr("g_y1", [S, H], F32)
    g_y2 = scr("g_y2", [S, H], F32)
    g_xln = scr("g_xln", [S, H], BF16)
    g_wq_s = scr("g_wq_s", [256, H], BF16)
    g_wk_s = scr("g_wk_s", [256, H], BF16)
    g_wv_s = scr("g_wv_s", [256, H], BF16)
    g_wo_s = scr("g_wo_s", [256, H], BF16)
    g_wq = nc.dram_tensor("g_wq", [H, H], BF16, kind="Internal",
                          addr_space="Shared").ap()
    g_wk = nc.dram_tensor("g_wk", [H, H], BF16, kind="Internal",
                          addr_space="Shared").ap()
    g_wv = nc.dram_tensor("g_wv", [H, H], BF16, kind="Internal",
                          addr_space="Shared").ap()
    g_wo = nc.dram_tensor("g_wo", [H, H], BF16, kind="Internal",
                          addr_space="Shared").ap()
    g_qT = scr("g_qT", [NH * 128, S], BF16)
    g_kT = scr("g_kT", [NH * 128, S], BF16)
    g_v = scr("g_v", [S, H], BF16)
    g_ctxT = scr("g_ctxT", [NH * 128, S], BF16)
    g_dbg_p = scr("g_dbg_p", [128, 2048], BF16)
    g_dbg_z = scr("g_dbg_z", [128, 512], F32)
    g_dbg_q = scr("g_dbg_q", [128, S], BF16)
    g_dbg_k = scr("g_dbg_k", [128, S], BF16)

    ctx = contextlib.ExitStack()
    with ctx:
        def sbt(name, shape, dtype, **kw):
            return ctx.enter_context(nc.sbuf_tensor(name, shape, dtype, **kw))

        arena = sbt("arena", [128, 51200], BF16, is_transpose_dest=True)
        ar_f32 = arena.bitcast(F32)
        ar_i32 = arena.bitcast(I32)
        wcol = [sbt(f"wcol{i}", [128, 2 * HC * 128], BF16) for i in range(2)]
        hbuf = [sbt(f"hbuf{i}", [128, 2048], BF16) for i in range(2)]
        idxb = [sbt(f"idxb{i}", [128, 1024], I32) for i in range(2)]
        mtmp = sbt("mtmp", [128, 1024], BF16)
        maccs = [sbt(f"macc{i}", [128, 1024], BF16) for i in range(2)]
        xtile = [sbt(f"xtile{i}", [128, 2048], F32) for i in range(2)]
        ybuf = [sbt(f"ybuf{i}", [128, 2048], F32) for i in range(2)]
        cosT = sbt("cosT", [128, S], BF16)
        ssinT = sbt("ssinT", [128, S], BF16)
        rtmp = sbt("rtmp", [128, 1024], F32)
        masks = sbt("masks", [128, 4 * 512], BF16)
        ones_bf = sbt("ones_bf", [128, 128], BF16)
        ones_f = sbt("ones_f", [1, 128], F32)
        lutb = sbt("lutb", [128, 16], F32)
        sgb = [sbt(f"sgb{i}", [128, 512], BF16) for i in range(2)]
        stats = sbt("stats", [128, 16], F32)
        qkbuf = [sbt(f"qkbuf{i}", [128, 2048], BF16) for i in range(2)]
        junk = arena[:, 49152:51200]

        psb = [ctx.enter_context(nc.psum_tensor(f"pb{i}", [128, 512], F32))
               for i in range(8)]

        class Sem:
            __slots__ = ("h", "n")
            _k = [0]

            def __init__(self, name):
                Sem._k[0] += 1
                self.h = ctx.enter_context(
                    nc.semaphore(f"{name}{Sem._k[0]}"))
                self.n = 0

        streams = {e: [] for e in ENGS}

        def op(eng, fn, waits=(), inc=None, inc_n=1):
            if inc is not None:
                inc.n += inc_n
            ws = tuple((s.h, v) for s, v in waits if v > 0)
            tgt = (inc.h, inc_n) if inc is not None else None

            def closure(e, fn=fn, ws=ws, tgt=tgt):
                for h, v in ws:
                    e.wait_ge(h, v)
                inst = fn(e)
                if tgt is not None:
                    inst.then_inc(*tgt)
            streams[eng].append(closure)

        def wait(eng, *waits):
            ws = tuple((s.h, v) for s, v in waits if v > 0)

            def closure(e, ws=ws):
                for h, v in ws:
                    e.wait_ge(h, v)
            streams[eng].append(closure)

        def spair(name):
            return (Sem(name), Sem(name))

        c_v, c_s, c_t, c_g = Sem("cv"), Sem("cs"), Sem("ct"), Sem("cg")

        d_lut = Sem("dlut")
        lutb_ready = 0
        tables_done_s = 0
        masks_done_g = 0
        def emit_mask_consts():
            op("gpsimd", lambda e: e.memset(ones_bf[:], 1.0), inc=c_g)
            op("gpsimd", lambda e: e.memset(masks[:], 1.0), inc=c_g)
            for j in range(4):
                op("gpsimd",
                   lambda e, j=j: e.affine_select(
                       masks[:, j * 512:(j + 1) * 512],
                       masks[:, j * 512:(j + 1) * 512],
                       [[1, 512]], ALU.is_ge, 0.0,
                       base=-128 * j, channel_multiplier=-1),
                   inc=c_g)
            return c_g.n

        if "p0" in phases:
            op("gpsimd", lambda e: e.memset(ones_f[:], 1.0), inc=c_g)
            g_consts = c_g.n
            op("sync", lambda e: e.dma_start(out=lutb[0:1, :], in_=lut_in),
               inc=d_lut, inc_n=16)
            op("tensor",
               lambda e: e.matmul(psb[7][:, 0:16], ones_f[:], lutb[0:1, :],
                                  start=True, stop=True),
               waits=[(d_lut, 16), (c_g, g_consts)], inc=c_t)
            op("scalar", lambda e: e.copy(lutb[:], psb[7][:, 0:16]),
               waits=[(c_t, c_t.n)], inc=c_s)
            lutb_ready = c_s.n

            tmpf = ar_f32[:, 0:4096]
            tmpf2 = ar_f32[:, 4096:8192]
            ki = ar_i32[:, 8192:12288]
            kf = ar_f32[:, 12288:16384]
            sinf = ar_f32[:, 16384:20480]
            op("gpsimd",
               lambda e: e.iota(stats[:, 7:8], [[0, 1]], base=0,
                                channel_multiplier=1,
                                allow_small_or_imprecise_dtypes=True),
               inc=c_g)
            op("gpsimd",
               lambda e: e.tensor_scalar(stats[:, 6:7], stats[:, 7:8], 64.0,
                                         64.0, ALU.is_ge, ALU.mult))
            op("gpsimd",
               lambda e: e.tensor_tensor(stats[:, 7:8], stats[:, 7:8],
                                         stats[:, 6:7], ALU.subtract),
               inc=c_g)
            g_rr = c_g.n
            op("scalar",
               lambda e: e.activation(stats[:, 6:7], stats[:, 7:8], AF.Exp,
                                      scale=f32(-LN1E4 / 64.0)),
               waits=[(c_g, g_rr)], inc=c_s)
            s_invf = c_s.n
            op("gpsimd",
               lambda e: e.iota(tmpf, [[1, 4096]], base=0,
                                channel_multiplier=0,
                                allow_small_or_imprecise_dtypes=True),
               inc=c_g)
            g_t = c_g.n

            def sin_table(dst, shift):
                w0 = [(c_g, g_t), (c_s, max(s_invf, c_s.n))]
                if shift == 0.0:
                    op("vector",
                       lambda e: e.tensor_single_scalar(
                           tmpf2, tmpf, stats[:, 6:7], ALU.mult),
                       waits=w0)
                else:
                    op("vector",
                       lambda e, sh=shift: e.tensor_scalar(
                           tmpf2, tmpf, stats[:, 6:7], sh, ALU.mult, ALU.add),
                       waits=w0)
                op("vector", lambda e: e.tensor_single_scalar(
                    sinf, tmpf2, 1.0 / TWO_PI, ALU.mult))
                op("vector", lambda e: e.tensor_copy(ki, sinf))
                op("vector", lambda e: e.tensor_copy(kf, ki))
                op("vector", lambda e: e.scalar_tensor_tensor(
                    sinf, kf, -C1, tmpf2, ALU.mult, ALU.add))
                op("vector", lambda e: e.scalar_tensor_tensor(
                    sinf, kf, -C2, sinf, ALU.mult, ALU.add), inc=c_v)
                vr = c_v.n
                op("scalar", lambda e, d=dst: e.activation(d, sinf, AF.Sin),
                   waits=[(c_v, vr)], inc=c_s)
                return c_s.n

            sin_table(cosT[:], math.pi / 2)
            s_tab = sin_table(ssinT[:], 0.0)
            tables_done_s = s_tab
            op("vector",
               lambda e: e.tensor_single_scalar(ssinT[0:64, :],
                                                ssinT[0:64, :],
                                                -1.0, ALU.mult),
               waits=[(c_s, s_tab)], inc=c_v)
            tables_done_v = c_v.n
        else:
            tables_done_v = 0

        CH = 1024
        mat_jobs = []
        piece_last_job = []
        for b, (c0, w) in enumerate(PIECES):
            for t in range(4):
                mat_jobs.append(
                    (guidxT_in[t * 128:(t + 1) * 128, c0:c0 + w],
                     g_sh[b][t * 128:(t + 1) * 128, :], w))
            piece_last_job.append(len(mat_jobs) - 1)
        for t in range(6):
            for c0 in (0, 1024):
                mat_jobs.append(
                    (didxT_in[t * 128:(t + 1) * 128, c0:c0 + CH],
                     g_shd[t * 128:(t + 1) * 128, c0:c0 + CH], CH))
        piece_last_job.append(len(mat_jobs) - 1)

        d_idx = spair("didx")
        w_wm = Sem("wwm")
        w_cst = Sem("wcst")
        cc_sem = Sem("ccw")

        if "p0" in phases:
            for w_in, w_sh in ((wqs_in, g_wq_s), (wks_in, g_wk_s),
                               (wvs_in, g_wv_s), (wos_in, g_wo_s)):
                op("gpsimd", lambda e, a=w_in, b=w_sh: e.dma_start(
                    out=b, in_=a), inc=w_cst, inc_n=16)
            for piece in range(NGU + 1):
                src = g_sh[piece].opt() if piece < NGU else g_shd.opt()
                dst = g_gup[piece].opt() if piece < NGU else g_wdp.opt()
                op("gpsimd", lambda e, src=src, dst=dst: e.collective_compute(
                    "AllGather", ALU.bypass,
                    replica_groups=[[0, 1, 2, 3, 4, 5, 6, 7]],
                    ins=[src], outs=[dst]),
                   waits=[(w_wm, 16 * (piece_last_job[piece] + 1))],
                   inc=cc_sem)
            for w_sh, w_full in ((g_wq_s, g_wq), (g_wk_s, g_wk),
                                 (g_wv_s, g_wv), (g_wo_s, g_wo)):
                op("gpsimd", lambda e, a=w_sh, b=w_full: e.collective_compute(
                    "AllGather", ALU.bypass,
                    replica_groups=[[0, 1, 2, 3, 4, 5, 6, 7]],
                    ins=[a.opt()], outs=[b.opt()]),
                   waits=[(w_cst, 64)], inc=cc_sem)
            op("gpsimd", lambda e: e.collective_compute(
                "AllReduce", ALU.add,
                replica_groups=[[0, 1, 2, 3, 4, 5, 6, 7]],
                ins=[g_bar_i.opt()], outs=[g_bar_o.opt()]),
               inc=cc_sem)
        cc_all = cc_sem.n

        mat_state = {"k": 0, "j": 0, "read_done": {}}
        NMJ = len(mat_jobs) if "p0" in phases else 0

        def emit_idx_load(k):
            in_ap, _, w = mat_jobs[k]
            rd = mat_state["read_done"].get(k - 2, 0)
            op("scalar", lambda e, a=in_ap, b=idxb[k % 2], w=w:
               e.dma_start(out=b[:, 0:w], in_=a),
               waits=[(c_v, rd)], inc=d_idx[k % 2], inc_n=16)

        def pump(n):
            for _ in range(n):
                k = mat_state["k"]
                if k >= NMJ:
                    return
                j = mat_state["j"]
                in_ap, out_ap, w = mat_jobs[k]
                buf = idxb[k % 2]
                mc = maccs[k % 2]
                if j == 0:
                    op("vector", lambda e, buf=buf, mc=mc, w=w:
                       e.tensor_scalar(mc[:, 0:w], buf[:, 0:w], 0.0,
                                       lutb[:, 0:1], ALU.is_equal, ALU.mult),
                       waits=[(d_idx[k % 2], 16 * (k // 2 + 1)),
                              (c_s, lutb_ready),
                              (w_wm, 16 * max(0, k - 1))])
                elif j % 2 == 1:
                    jj = (j + 1) // 2
                    op("vector", lambda e, buf=buf, jj=jj, w=w:
                       e.tensor_scalar(mtmp[:, 0:w], buf[:, 0:w], float(jj),
                                       lutb[:, jj:jj + 1],
                                       ALU.is_equal, ALU.mult),
                       inc=(c_v if j == 29 else None))
                    if j == 29:
                        mat_state["read_done"][k] = c_v.n
                else:
                    op("vector", lambda e, mc=mc, w=w: e.tensor_tensor(
                        mc[:, 0:w], mc[:, 0:w], mtmp[:, 0:w], ALU.add),
                       inc=(c_v if j == 30 else None))
                if j == 30:
                    op("scalar", lambda e, o=out_ap, mc=mc, w=w: e.dma_start(
                        out=o, in_=mc[:, 0:w]),
                       waits=[(c_v, c_v.n)], inc=w_wm, inc_n=16)
                    if k + 2 < NMJ:
                        emit_idx_load(k + 2)
                    mat_state["k"] = k + 1
                    mat_state["j"] = 0
                else:
                    mat_state["j"] = j + 1

        w_xln = spair("wxln")
        ln_state = {"i": 0, "tile_store": {}}

        def ln_a(i, src_ap, src_waits):
            cb = (i % 2) * 8
            op("scalar", lambda e, a=src_ap, cb=cb: e.activation(
                junk[:], a, AF.Copy, accum_out=stats[:, cb:cb + 1]),
               waits=src_waits, inc=c_s)
            op("scalar", lambda e, a=src_ap, cb=cb: e.activation(
                junk[:], a, AF.Square, accum_out=stats[:, cb + 1:cb + 2]),
               inc=c_s)
            op("scalar", lambda e, cb=cb: e.mul(
                stats[:, cb + 2:cb + 3], stats[:, cb:cb + 1], 1.0 / H))
            op("scalar", lambda e, cb=cb: e.mul(
                stats[:, cb + 3:cb + 4], stats[:, cb + 1:cb + 2], 1.0 / H),
               inc=c_s)
            return c_s.n

        def ln_b(i, src_ap, s_stats, eng="gpsimd"):
            cx = c_g if eng == "gpsimd" else c_v
            cb = (i % 2) * 8
            op(eng, lambda e, cb=cb: e.tensor_tensor(
                stats[:, cb + 7:cb + 8], stats[:, cb + 2:cb + 3],
                stats[:, cb + 2:cb + 3], ALU.mult),
               waits=[(c_s, s_stats)], inc=cx)
            wait(eng, (cx, cx.n))
            op(eng, lambda e, cb=cb: e.tensor_tensor(
                stats[:, cb + 3:cb + 4], stats[:, cb + 3:cb + 4],
                stats[:, cb + 7:cb + 8], ALU.subtract), inc=cx)
            wait(eng, (cx, cx.n))
            op(eng, lambda e, cb=cb: e.tensor_single_scalar(
                stats[:, cb + 5:cb + 6], stats[:, cb + 2:cb + 3],
                -1.0, ALU.mult), inc=cx)
            g_st = cx.n
            op("scalar", lambda e, cb=cb: e.activation(
                stats[:, cb + 4:cb + 5], stats[:, cb + 3:cb + 4], AF.Ln,
                bias=f32(1e-5)),
               waits=[(cx, g_st)])
            op("scalar", lambda e, cb=cb: e.activation(
                stats[:, cb + 4:cb + 5], stats[:, cb + 4:cb + 5], AF.Exp,
                scale=f32(-0.5)),
               inc=c_s)
            s_rstd = c_s.n
            dst = hbuf[i % 2]
            ws = w_xln[i % 2]
            op(eng, lambda e, a=src_ap, d=dst, cb=cb: e.tensor_scalar(
                d[:], a, stats[:, cb + 5:cb + 6], stats[:, cb + 4:cb + 5],
                ALU.add, ALU.mult),
               waits=[(c_s, s_rstd), (ws, ws.n)], inc=cx)
            g_norm = cx.n
            row = (i % ST) * 128
            op("scalar", lambda e, r=row, d=dst: e.dma_start(
                out=g_xln[r:r + 128, :], in_=d[:]),
               waits=[(cx, g_norm)], inc=ws, inc_n=16)
            ln_state["tile_store"][i % ST] = (ws, ws.n)
            return (cx, cx.n)

        def emit_ln_tile(src_ap, src_waits, eng="gpsimd"):
            i = ln_state["i"]
            ln_state["i"] = i + 1
            s_stats = ln_a(i, src_ap, src_waits)
            return ln_b(i, src_ap, s_stats, eng)

        d_x = spair("dx")
        w_wb = Sem("wwb")
        if "p1" in phases:
            if NMJ:
                emit_idx_load(0)
                emit_idx_load(1)
            x_rd = {}
            pend = {}
            for i in range(ST):
                ld = d_x[i % 2]
                buf = xtile[i % 2]
                w = x_rd.get(i - 2)
                op("sync" if i < 16 else "scalar",
                   lambda e, i=i, b=buf: e.dma_start(
                       out=b[:], in_=x_in[i * 128:(i + 1) * 128, :]),
                   waits=[w] if w else [], inc=ld, inc_n=16)
                ln_state["i"] = i + 1
                pend[i] = (buf, ln_a(i, buf[:], [(ld, ld.n)]))
                if i >= 1:
                    b1, s1 = pend.pop(i - 1)
                    x_rd[i - 1] = ln_b(i - 1, b1[:], s1, eng=LN1_ENG)
                pump(31 if mat_state["k"] < 8 else (12 if i < 16 else 16))
            b1, s1 = pend.pop(ST - 1)
            x_rd[ST - 1] = ln_b(ST - 1, b1[:], s1, eng=LN1_ENG)

        if "p0" in phases:
            masks_done_g = emit_mask_consts()
        if "t_gup" in DEBUG_TAPS and "p0" in phases:
            t_gup = [nc.dram_tensor(f"t_gup{b}", [4096, w], BF16,
                                    kind="ExternalOutput").ap()
                     for b, (c0, w) in enumerate(PIECES)]
            t_wdp = nc.dram_tensor("t_wdp", [6144, H], BF16,
                                   kind="ExternalOutput").ap()
            for b in range(NGU):
                op("scalar", lambda e, b=b: e.dma_start(
                    out=t_gup[b], in_=g_gup[b]),
                   waits=[(cc_sem, b + 1)], inc=w_wb, inc_n=16)
            op("scalar", lambda e: e.dma_start(out=t_wdp, in_=g_wdp),
               waits=[(cc_sem, 7)], inc=w_wb, inc_n=16)

        d_xlt = Sem("dxlt")
        d_wc = spair("dwc")
        w_h = spair("wh")

        def emit_gate_up(tag, side_jobs, arena_free):
            xlt = arena[:, 0:HC * 2048].rearrange("p (hc s) -> p hc s", hc=HC)
            ts = dict(ln_state["tile_store"])
            xln_half = [[ts[i] for i in (14, 15) if i in ts],
                        [ts[i] for i in (30, 31) if i in ts]]
            st_wcol, st_evac, st_hst = {}, {}, {}
            for sh in range(2):
                ct_now = c_t.n
                if sh == 0:
                    ws = xln_half[0] + list(arena_free)
                else:
                    ws = xln_half[1] + [(c_t, ct_now)]
                op("sync",
                   lambda e, sh=sh: e.dma_start_transpose(
                       xlt[:, :, :],
                       g_xln[sh * 2048:(sh + 1) * 2048, :]),
                   waits=ws, inc=d_xlt, inc_n=16)
                xlt_done = d_xlt.n
                for oc in range(OC):
                    k = sh * OC + oc
                    wsem = d_wc[k % 2]
                    wb = wcol[k % 2]
                    cb = next(p for p, (c0, pw) in enumerate(PIECES)
                              if c0 <= oc * 128 < c0 + pw)
                    co = oc * 128 - PIECES[cb][0]
                    blk_ws = [(cc_sem, cb + 1)] if sh == 0 else []
                    op("sync", lambda e, cb=cb, co=co, wb=wb: e.dma_start(
                        out=wb[:, 0:HC * 128].rearrange(
                            "p (hc o) -> p hc o", hc=HC),
                        in_=g_gup[cb][0:2048].rearrange(
                            "(hc p) o -> p hc o", p=128)[:, :, co:co + 128]),
                       waits=blk_ws + [(c_t, st_wcol.get(k - 2, 0))],
                       inc=wsem, inc_n=16)
                    op("sync", lambda e, cb=cb, co=co, wb=wb: e.dma_start(
                        out=wb[:, HC * 128:2 * HC * 128].rearrange(
                            "p (hc o) -> p hc o", hc=HC),
                        in_=g_gup[cb][2048:4096].rearrange(
                            "(hc p) o -> p hc o", p=128)[:, :, co:co + 128]),
                       inc=wsem, inc_n=16)
                    w_ready = wsem.n
                    for sc in range(4):
                        slot = k * 4 + sc
                        pg = psb[slot % 4]
                        pu = psb[4 + slot % 4]
                        mw = [(wsem, w_ready),
                              (c_v, st_evac.get(slot - 4, 0))]
                        if slot == 3:
                            mw.append((c_s, lutb_ready))
                        if oc == 0 and sc == 0:
                            mw.append((d_xlt, xlt_done))

                        def mm(e, xlt=xlt, wb=wb, pg=pg, pu=pu, sc=sc):
                            for h2 in range(HC):
                                e.matmul(pg[:], wb[:, h2 * 128:(h2 + 1) * 128],
                                         xlt[:, h2, sc * 512:(sc + 1) * 512],
                                         start=(h2 == 0), stop=(h2 == HC - 1))
                            r = None
                            for h2 in range(HC):
                                r = e.matmul(
                                    pu[:],
                                    wb[:, (HC + h2) * 128:(HC + h2 + 1) * 128],
                                    xlt[:, h2, sc * 512:(sc + 1) * 512],
                                    start=(h2 == 0), stop=(h2 == HC - 1))
                            return r
                        op("tensor", mm, waits=mw, inc=c_t)
                        t_mm = c_t.n
                        st_wcol[k] = t_mm
                        sg = sgb[slot % 2]
                        op("scalar", lambda e, pg=pg, sg=sg: e.activation(
                            sg[:], pg[:], AF.Silu),
                           waits=[(c_t, t_mm),
                                  (c_v, st_evac.get(slot - 2, 0))],
                           inc=c_s)
                        s_silu = c_s.n
                        hb = hbuf[k % 2]
                        hw = w_h[k % 2]
                        vw = [(c_s, s_silu)]
                        if sc == 0:
                            vw.append((hw, hw.n))
                            if k < 2:
                                vw.append((w_xln[0], w_xln[0].n))
                                vw.append((w_xln[1], w_xln[1].n))
                        op("vector",
                           lambda e, hb=hb, sg=sg, pu=pu, sc=sc:
                           e.tensor_tensor(hb[:, sc * 512:(sc + 1) * 512],
                                           sg[:], pu[:], ALU.mult),
                           waits=vw, inc=c_v)
                        st_evac[slot] = c_v.n
                        pump(6)
                    v_h = c_v.n
                    op("scalar", lambda e, oc=oc, sh=sh, hb=hbuf[k % 2]:
                       e.dma_start(
                           out=g_h[oc * 128:(oc + 1) * 128,
                                   sh * 2048:(sh + 1) * 2048],
                           in_=hb[:]),
                       waits=[(c_v, v_h)], inc=hw, inc_n=16)
                    st_hst[k] = hw.n
            return c_t.n

        d_wd = Sem("dwd")
        d_hc = spair("dhc")
        d_xq = spair("dxq")
        w_y = spair("wy")
        w_out = spair("wout")

        down_store = {}

        def emit_down(tag, res_dram, out_dram, wsems, pre_waits, arena_free_t,
                      ybuf_war, tile_cb=None):
            wd_sb = arena[:, 0:OC * 512].rearrange("p (oc d) -> p oc d", oc=OC)
            hb0 = OC * 512
            hcols = [arena[:, hb0 + i * O: hb0 + (i + 1) * O]
                     .rearrange("p (oc s) -> p oc s", oc=OC) for i in range(2)]
            st_hcol, st_evac, st_xq = {}, {}, {}
            wd_prev_used = arena_free_t
            h_now = [(w_h[0], w_h[0].n), (w_h[1], w_h[1].n)]
            OCS = [(0, 11), (11, 11), (22, 11), (33, 10)]
            for dq in range(4):
                wd_subs = []
                for sj, (o0, on) in enumerate(OCS):
                    op("sync", lambda e, dq=dq, o0=o0, on=on: e.dma_start(
                        out=wd_sb[:, o0:o0 + on, :],
                        in_=g_wdp.rearrange("(oc p) d -> p oc d", p=128)[
                            :, o0:o0 + on, dq * 512:(dq + 1) * 512]),
                       waits=((list(pre_waits) + h_now if dq == 0 else []) +
                              [(c_t, wd_prev_used)]) if sj == 0 else [],
                       inc=d_wd, inc_n=16)
                    wd_subs.append(d_wd.n)
                wd_ready = d_wd.n
                for st in range(ST):
                    k = dq * ST + st
                    hsem = d_hc[k % 2]
                    op("sync", lambda e, st=st, hc=hcols[k % 2]: e.dma_start(
                        out=hc,
                        in_=g_h.rearrange("(oc p) s -> p oc s", p=128)[
                            :, :, st * 128:(st + 1) * 128]),
                       waits=[(c_t, st_hcol.get(
                           k - 2, arena_free_t if k < 2 else 0))],
                       inc=hsem, inc_n=16)
                    h_ready = hsem.n
                    xq = d_xq[k % 2]
                    xb = xtile[k % 2]
                    op("sync", lambda e, st=st, dq=dq, xb=xb: e.dma_start(
                        out=xb[:, 0:512],
                        in_=res_dram[st * 128:(st + 1) * 128,
                                     dq * 512:(dq + 1) * 512]),
                       waits=[(c_v, st_xq.get(k - 2, 0))], inc=xq, inc_n=16)
                    x_ready = xq.n
                    ps = psb[4 + (k % 2)]

                    subw = ([(o0, wd_subs[sj]) for sj, (o0, on) in
                             enumerate(OCS)] if st == 0 else [])

                    def mm(e, hc=hcols[k % 2], ps=ps, subw=subw):
                        r = None
                        bnd = dict(subw)
                        for oc in range(OC):
                            if oc in bnd:
                                e.wait_ge(d_wd.h, bnd[oc])
                            r = e.matmul(ps[:], hc[:, oc, :], wd_sb[:, oc, :],
                                         start=(oc == 0), stop=(oc == OC - 1))
                        return r
                    op("tensor", mm,
                       waits=([(d_wd, wd_ready)] if st > 0 else []) +
                             [(hsem, h_ready),
                              (c_v, st_evac.get(k - 2, 0))],
                       inc=c_t)
                    t_mm = c_t.n
                    st_hcol[k] = t_mm
                    if st == ST - 1:
                        wd_prev_used = t_mm
                    yb = ybuf[st % 2]
                    ysem = wsems[k % 2]
                    vw = [(c_t, t_mm), (xq, x_ready), (ysem, ysem.n)]
                    if k < 2 and (k % 2) in ybuf_war:
                        vw.append(ybuf_war[k % 2])
                    op("vector", lambda e, ps=ps, xb=xb, yb=yb, dq=dq:
                       e.scalar_tensor_tensor(
                           yb[:, dq * 512:(dq + 1) * 512], ps[:], 0.5,
                           xb[:, 0:512], ALU.mult, ALU.add),
                       waits=vw, inc=c_v)
                    st_evac[k] = c_v.n
                    st_xq[k] = c_v.n
                    v_ev = c_v.n
                    op("scalar", lambda e, st=st, dq=dq, yb=yb: e.dma_start(
                        out=out_dram[st * 128:(st + 1) * 128,
                                     dq * 512:(dq + 1) * 512],
                        in_=yb[:, dq * 512:(dq + 1) * 512]),
                       waits=[(c_v, v_ev)], inc=ysem, inc_n=16)
                    down_store[(tag, dq, st)] = (ysem, ysem.n)
                    if tile_cb is not None and dq == 3:
                        tile_cb(st)
            return c_t.n

        d_yr = spair("dyr")
        ln2_state = {"y1_rd": {}, "pend": {}}

        def ln2_tile(i):
            ld = d_yr[i % 2]
            y4buf = [ar_f32[:, 16512:18560], ar_f32[:, 18560:20608]]
            buf = y4buf[i % 2]
            y1_rd = ln2_state["y1_rd"]
            pend4 = ln2_state["pend"]
            op("gpsimd", lambda e, i=i, b=buf: e.dma_start(
                out=b, in_=g_y1[i * 128:(i + 1) * 128, :]),
               waits=[down_store[("f1", 3, i)]] +
                     ([y1_rd[i - 2]] if i - 2 in y1_rd else []),
               inc=ld, inc_n=16)
            ln_state["i"] += 1
            li = ln_state["i"] - 1
            pend4[i] = (li, buf, ln_a(li, buf, [(ld, ld.n)]))
            if i >= 1:
                li1, b1, s1 = pend4.pop(i - 1)
                y1_rd[i - 1] = ln_b(li1, b1, s1, eng="vector")
            if i == ST - 1:
                li1, b1, s1 = pend4.pop(ST - 1)
                y1_rd[ST - 1] = ln_b(li1, b1, s1, eng="vector")

        p2_end = 0
        if "p2" in phases:
            p2_end = emit_gate_up(
                "f1", [], [(c_s, tables_done_s), (c_v, tables_done_v)])
        while mat_state["k"] < NMJ:
            pump(31)
        p3_end = p2_end
        if "p3" in phases:
            p3_end = emit_down("f1", x_in, g_y1, w_y,
                               [(cc_sem, min(NGU + 1, cc_all))], p2_end, {},
                               tile_cb=(ln2_tile if "p4" in phases
                                        else None))
            y1_done = [(w_y[0], w_y[0].n), (w_y[1], w_y[1].n)]



        d_wv = Sem("dwv")
        w_qk = spair("wqk")
        w_v = spair("wv")
        p5_end_t = p3_end
        if "p5" in phases:
            xlt = arena[:, 0:HC * 2048].rearrange("p (hc s) -> p hc s", hc=HC)
            wvslabs = [arena[:, 32768 + i * 8192:32768 + (i + 1) * 8192]
                       .rearrange("p (hc d) -> p hc d", hc=HC)
                       for i in range(2)]
            st_wcol, st_rope = {}, {}
            vev_mm, vev_ev = {}, {}
            qk_war = {}
            ts5 = dict(ln_state["tile_store"])
            xln_half5 = [[ts5[i] for i in (14, 15) if i in ts5],
                         [ts5[i] for i in (30, 31) if i in ts5]]
            wb0 = wcol[0]
            op("sync", lambda e, wb=wb0: e.dma_start(
                out=wb[:, 0:HC * 128].rearrange("p (hc d) -> p hc d", hc=HC),
                in_=g_wq.rearrange("(hc p) d -> p hc d", p=128)[:, :, 0:128]),
               waits=[(c_t, p2_end), (cc_sem, cc_all)],
               inc=d_wc[0], inc_n=16)
            op("sync", lambda e, wb=wb0: e.dma_start(
                out=wb[:, HC * 128:2 * HC * 128].rearrange(
                    "p (hc d) -> p hc d", hc=HC),
                in_=g_wk.rearrange("(hc p) d -> p hc d", p=128)[:, :, 0:128]),
               inc=d_wc[0], inc_n=16)
            prefetched = {"done": False}
            for sh in range(2):
                ct_now5 = c_t.n
                if sh == 0:
                    ws = xln_half5[0] + [(c_t, p3_end)]
                else:
                    ws = xln_half5[1] + [(c_t, ct_now5)]
                op("sync",
                   lambda e, sh=sh: e.dma_start_transpose(
                       xlt[:, :, :],
                       g_xln[sh * 2048:(sh + 1) * 2048, :]),
                   waits=ws, inc=d_xlt, inc_n=16)
                xlt_done = d_xlt.n
                for dc in range(NH):
                    k = sh * NH + dc
                    wsem = d_wc[k % 2]
                    wb = wcol[k % 2]
                    if k == 0:
                        w_ready = wsem.n
                    else:
                        op("sync", lambda e, dc=dc, wb=wb: e.dma_start(
                            out=wb[:, 0:HC * 128].rearrange(
                                "p (hc d) -> p hc d", hc=HC),
                            in_=g_wq.rearrange("(hc p) d -> p hc d", p=128)[
                                :, :, dc * 128:(dc + 1) * 128]),
                           waits=[(c_t, st_wcol.get(
                               k - 2, p2_end if k < 2 else 0))],
                           inc=wsem, inc_n=16)
                        op("sync", lambda e, dc=dc, wb=wb: e.dma_start(
                            out=wb[:, HC * 128:2 * HC * 128].rearrange(
                                "p (hc d) -> p hc d", hc=HC),
                            in_=g_wk.rearrange("(hc p) d -> p hc d", p=128)[
                                :, :, dc * 128:(dc + 1) * 128]),
                           inc=wsem, inc_n=16)
                        w_ready = wsem.n
                    for qk in range(2):
                        qbuf = qkbuf[qk]
                        wqksem = w_qk[qk]
                        for sc in range(4):
                            slot = (k * 2 + qk) * 4 + sc
                            ps = psb[(slot % 2) + qk * 2]
                            mw = [(wsem, w_ready),
                                  (c_v, st_rope.get((qk, slot - 2), 0))]
                            if dc == 0 and qk == 0 and sc == 0:
                                mw.append((d_xlt, xlt_done))

                            def mm(e, xlt=xlt, wb=wb, ps=ps, sc=sc, qk=qk):
                                r = None
                                for h2 in range(HC):
                                    r = e.matmul(
                                        ps[:],
                                        wb[:, (qk * HC + h2) * 128:
                                           (qk * HC + h2 + 1) * 128],
                                        xlt[:, h2, sc * 512:(sc + 1) * 512],
                                        start=(h2 == 0), stop=(h2 == HC - 1))
                                return r
                            op("tensor", mm, waits=mw, inc=c_t)
                            t_mm = c_t.n
                            st_wcol[k] = t_mm
                            gofs = sh * 2048 + sc * 512
                            t1 = rtmp[:, 0:512]
                            t2 = rtmp[:, 512:1024]
                            vw = [(c_t, t_mm)]
                            if sc == 0:
                                vw.append((wqksem, qk_war.get((qk, k - 2), 0)))
                            if k == 0 and qk == 0 and sc == 0:
                                vw.append((c_v, tables_done_v))
                            op("vector",
                               lambda e, ps=ps, g=gofs: e.tensor_tensor(
                                   t1, ps[:], cosT[:, g:g + 512], ALU.mult),
                               waits=vw)
                            op("vector",
                               lambda e, ps=ps, g=gofs: e.tensor_tensor(
                                   t2[0:64, :], ps[64:128, :],
                                   ssinT[0:64, g:g + 512], ALU.mult))
                            op("vector",
                               lambda e, ps=ps, g=gofs: e.tensor_tensor(
                                   t2[64:128, :], ps[0:64, :],
                                   ssinT[64:128, g:g + 512], ALU.mult))
                            op("vector", lambda e, qbuf=qbuf, sc=sc:
                               e.tensor_tensor(
                                   qbuf[:, sc * 512:(sc + 1) * 512],
                                   t1, t2, ALU.add),
                               inc=c_v)
                            st_rope[(qk, slot)] = c_v.n
                        v_r = c_v.n
                        dstg = g_qT if qk == 0 else g_kT
                        op("scalar",
                           lambda e, d=dstg, dc=dc, sh=sh, qbuf=qbuf:
                           e.dma_start(
                               out=d[dc * 128:(dc + 1) * 128,
                                     sh * 2048:(sh + 1) * 2048],
                               in_=qbuf[:]),
                           waits=[(c_v, v_r)], inc=wqksem, inc_n=16)
                        qk_war[(qk, k)] = wqksem.n
                for dc4 in range(4):
                    kv = sh * 4 + dc4
                    wvslab = wvslabs[kv % 2]
                    wv_extra = ([(w_xln[0], w_xln[0].n),
                                 (w_xln[1], w_xln[1].n),
                                 (c_g, c_g.n), (c_v, c_v.n)]
                                if kv < 2 else [])
                    op("sync", lambda e, dc4=dc4, wvslab=wvslab: e.dma_start(
                        out=wvslab,
                        in_=g_wv.rearrange("(hc p) d -> p hc d", p=128)[
                            :, :, dc4 * 512:(dc4 + 1) * 512]),
                       waits=wv_extra + [(c_t, vev_mm.get(kv - 2, 0))],
                       inc=d_wv, inc_n=16)
                    wv_ready = d_wv.n
                    for st in range(16):
                        kk = kv * 16 + st
                        ps = psb[4 + kk % 2]
                        mw = [(d_wv, wv_ready),
                              (c_s, vev_ev.get(kk - 2, 0))]

                        def mm(e, xlt=xlt, st=st, ps=ps, wvslab=wvslab):
                            r = None
                            for h2 in range(HC):
                                r = e.matmul(
                                    ps[:],
                                    xlt[:, h2, st * 128:(st + 1) * 128],
                                    wvslab[:, h2, :],
                                    start=(h2 == 0), stop=(h2 == HC - 1))
                            return r
                        op("tensor", mm, waits=mw, inc=c_t)
                        t_mm = c_t.n
                        if st == 15:
                            vev_mm[kv] = t_mm
                        sg = sgb[kk % 2]
                        wvs = w_v[kk % 2]
                        op("scalar", lambda e, ps=ps, sg=sg: e.copy(
                            sg[:], ps[:]),
                           waits=[(c_t, t_mm), (wvs, wvs.n)], inc=c_s)
                        vev_ev[kk] = c_s.n
                        s_ev = c_s.n
                        row = sh * 2048 + st * 128
                        op("scalar", lambda e, r=row, dc4=dc4, sg=sg:
                           e.dma_start(
                               out=g_v[r:r + 128, dc4 * 512:(dc4 + 1) * 512],
                               in_=sg[:]),
                           waits=[(c_s, s_ev)], inc=wvs, inc_n=16)
            p5_end_t = c_t.n
            qk_done = [(w_qk[0], w_qk[0].n), (w_qk[1], w_qk[1].n)]
            v_done = [(w_v[0], w_v[0].n), (w_v[1], w_v[1].n)]

        d_hd = spair("dhd")
        w_cx = spair("wcx")
        p6_end_t = p5_end_t
        if "p6" in phases:
            AH = [arena[:, i * 4096:(i + 1) * 4096] for i in range(6)]
            pT0 = 24576

            def pTs(kt):
                return arena[:, pT0 + kt * 512:pT0 + (kt + 1) * 512]
            ctxb = [arena[:, 40960 + i * 4096:40960 + (i + 1) * 4096]
                    for i in range(2)]
            head_end = {}
            cx_st_war = {}
            zc_last_ev = {0: 0, 1: 0}
            for h in range(NH):
                p = h % 2
                qh, kh, vh = AH[p], AH[2 + p], AH[4 + p]
                vhv = vh.rearrange("p (kc d) -> p kc d", kc=32)
                ld = d_hd[p]
                ws = [(c_t, head_end.get(h - 2, p5_end_t if h < 2 else 0))]
                if h == 0:
                    ws += qk_done + v_done + [(c_g, masks_done_g)]
                op("sync", lambda e, h=h, qh=qh: e.dma_start(
                    out=qh, in_=g_qT[h * 128:(h + 1) * 128, :]),
                   waits=ws, inc=ld, inc_n=16)
                op("sync", lambda e, h=h, kh=kh: e.dma_start(
                    out=kh, in_=g_kT[h * 128:(h + 1) * 128, :]),
                   inc=ld, inc_n=16)
                op("sync", lambda e, h=h, vhv=vhv: e.dma_start(
                    out=vhv,
                    in_=g_v.rearrange("(kc p) d -> p kc d", p=128)[
                        :, :, h * 128:(h + 1) * 128]),
                   inc=ld, inc_n=16)
                loads_rdy = ld.n
                if h == 0 and "g_dbg_q" in DEBUG_TAPS:
                    op("scalar", lambda e, qh=qh: e.dma_start(
                        out=g_dbg_q, in_=qh), waits=[(ld, loads_rdy)],
                       inc=w_wb, inc_n=16)
                    op("scalar", lambda e, kh=kh: e.dma_start(
                        out=g_dbg_k, in_=kh), waits=[(ld, loads_rdy)],
                       inc=w_wb, inc_n=16)
                ctx_end_prev = 0
                for qc in range(P6_NQC):
                    KT = 4 * (qc + 1)
                    if qc > 0:
                        wait("scalar", (c_t, ctx_end_prev))
                    psZ = psb[4 + (qc % 2)]
                    psC = psb[6 + (qc % 2)]
                    zc_war = zc_last_ev[qc % 2]
                    exp_done = {}
                    p_rdy = {}

                    def emit_zc(kt, first, last, psZ=psZ, psC=psC, vhv=vhv,
                                p_rdy=p_rdy):
                        op("tensor", lambda e, kt=kt, first=first, last=last:
                           e.matmul(psZ[:], ones_bf[:], pTs(kt),
                                    start=first, stop=last,
                                    skip_group_check=True),
                           waits=[p_rdy[kt]] +
                                 ([(c_v, zc_war)] if first else []))
                        op("tensor", lambda e, kt=kt, first=first, last=last:
                           e.matmul(psC[:], vhv[:, kt, :], pTs(kt),
                                    start=first, stop=last,
                                    skip_group_check=True),
                           inc=(c_t if last else None))

                    for kt in range(KT):
                        bank = psb[kt % 2]
                        mw = [(c_s, exp_done.get(kt - 2, 0))]
                        if P6_SERIAL:
                            mw = [(c_s, c_s.n), (c_v, c_v.n)]
                        if kt == 0 and qc == 0:
                            mw.append((ld, loads_rdy))
                        op("tensor",
                           lambda e, bank=bank, kt=kt, qc=qc, kh=kh, qh=qh:
                           e.matmul(bank[:], kh[:, kt * 128:(kt + 1) * 128],
                                    qh[:, qc * 512:(qc + 1) * 512],
                                    start=True, stop=True,
                                    skip_group_check=True),
                           waits=mw, inc=c_t)
                        t_sc = c_t.n
                        op("scalar", lambda e, bank=bank, kt=kt: e.activation(
                            pTs(kt), bank[:], AF.Exp,
                            bias=f32(EXP_BIAS), scale=f32(EXP_SCALE)),
                           waits=[(c_t, t_sc)], inc=c_s)
                        exp_done[kt] = c_s.n
                        diag = kt - 4 * qc
                        if diag >= 0:
                            op("vector", lambda e, kt=kt, diag=diag:
                               e.tensor_tensor(
                                   pTs(kt), pTs(kt),
                                   masks[:, diag * 512:(diag + 1) * 512],
                                   ALU.mult),
                               waits=[(c_s, exp_done[kt])] +
                                     ([(c_g, masks_done_g)]
                                      if (h == 0 and qc == 0) else []),
                               inc=c_v)
                            p_rdy[kt] = (c_v, c_v.n)
                        else:
                            p_rdy[kt] = (c_s, exp_done[kt])
                        if kt >= 2:
                            emit_zc(kt - 2, (kt - 2) == 0, False)
                    emit_zc(KT - 2, KT == 2, False)
                    emit_zc(KT - 1, False, True)
                    ctx_end_prev = c_t.n
                    vw = [(c_t, ctx_end_prev)]
                    if qc == 0:
                        vw.append((w_cx[p], cx_st_war.get(("st", h - 2), 0)))
                    op("vector", lambda e, psZ=psZ: e.reciprocal(
                        rtmp[:, 0:512], psZ[:]), waits=vw)
                    op("vector", lambda e, psC=psC, cb=ctxb[p], qc=qc:
                       e.tensor_tensor(cb[:, qc * 512:(qc + 1) * 512],
                                       psC[:], rtmp[:, 0:512], ALU.mult),
                       inc=c_v)
                    cx_st_war[("ev", h, qc)] = c_v.n
                    zc_last_ev[qc % 2] = c_v.n
                    if h == 0 and qc == 0 and "g_dbg_p" in DEBUG_TAPS:
                        dv = c_v.n
                        op("scalar", lambda e: e.dma_start(
                            out=g_dbg_p, in_=arena[:, pT0:pT0 + 2048]),
                           waits=[(c_v, dv)], inc=w_wb, inc_n=16)
                        op("scalar", lambda e: e.dma_start(
                            out=g_dbg_z, in_=rtmp[:, 0:512]),
                           waits=[(c_v, dv)], inc=w_wb, inc_n=16)
                head_end[h] = c_t.n
                v_cx = c_v.n
                op("scalar", lambda e, h=h, cb=ctxb[p]: e.dma_start(
                    out=g_ctxT[h * 128:(h + 1) * 128, :], in_=cb),
                   waits=[(c_v, v_cx)], inc=w_cx[p], inc_n=16)
                cx_st_war[("st", h)] = w_cx[p].n
            p6_end_t = c_t.n
            cx_done = [(w_cx[0], w_cx[0].n), (w_cx[1], w_cx[1].n)]

        d_cx = spair("dcx")
        p7_end_t = p6_end_t
        ln3_g = {}
        if "p7" in phases:
            wo_sb = arena[:, 0:HC * 2048].rearrange("p (dc o) -> p dc o",
                                                    dc=HC)
            s_now = c_s.n
            for q4 in range(4):
                op("sync", lambda e, q4=q4: e.dma_start(
                    out=wo_sb[:, q4 * 4:(q4 + 1) * 4, :],
                    in_=g_wo.rearrange("(hc p) d -> p hc d", p=128)[
                        :, q4 * 4:(q4 + 1) * 4, :]),
                   waits=([(c_t, p6_end_t), (c_s, s_now)]
                          if q4 == 0 else []),
                   inc=d_wv, inc_n=16)
            wo_ready = d_wv.n
            st_ccol, st_evx, st_evk = {}, {}, {}
            p3_y = {w_y[0]: w_y[0].n, w_y[1]: w_y[1].n}
            for st in range(ST):
                csem = d_cx[st % 2]
                cc = qkbuf[st % 2]
                ccv = cc.rearrange("p (nh s) -> p nh s", nh=NH)
                op("sync", lambda e, st=st, ccv=ccv: e.dma_start(
                    out=ccv,
                    in_=g_ctxT.rearrange("(nh p) s -> p nh s", p=128)[
                        :, :, st * 128:(st + 1) * 128]),
                   waits=(cx_done if st == 0 else []) +
                         [(c_t, st_ccol.get(st - 2, 0))],
                   inc=csem, inc_n=16)
                cc_ready = csem.n
                ld = d_yr[st % 2]
                xb = xtile[st % 2]
                op("sync", lambda e, st=st, xb=xb: e.dma_start(
                    out=xb[:], in_=g_y1[st * 128:(st + 1) * 128, :]),
                   waits=[(c_v, st_evx.get(st - 2, 0))],
                   inc=ld, inc_n=16)
                x_ready = ld.n
                yb = ybuf[st % 2]
                wsy = w_y[st % 2]
                for oc4 in range(4):
                    kk = st * 4 + oc4
                    ps = psb[4 + kk % 2]

                    def mm(e, ccv=ccv, ps=ps, oc4=oc4):
                        r = None
                        for dc in range(HC):
                            r = e.matmul(ps[:], ccv[:, dc, :],
                                         wo_sb[:, dc,
                                               oc4 * 512:(oc4 + 1) * 512],
                                         start=(dc == 0), stop=(dc == HC - 1))
                        return r
                    op("tensor", mm,
                       waits=[(d_wv, wo_ready), (csem, cc_ready),
                              (c_v, st_evk.get(kk - 2, 0))],
                       inc=c_t)
                    t_mm = c_t.n
                    st_ccol[st] = t_mm
                    vw = [(c_t, t_mm), (ld, x_ready)]
                    if oc4 == 0:
                        vw.append((wsy, wsy.n if st >= 2 else p3_y[wsy]))
                        if st >= 2:
                            vw.append(ln3_g[st - 2])
                    op("vector", lambda e, ps=ps, xb=xb, yb=yb, oc4=oc4:
                       e.tensor_tensor(yb[:, oc4 * 512:(oc4 + 1) * 512],
                                       ps[:],
                                       xb[:, oc4 * 512:(oc4 + 1) * 512],
                                       ALU.add),
                       waits=vw, inc=c_v)
                    st_evk[kk] = c_v.n
                st_evx[st] = c_v.n
                v_ev = c_v.n
                op("scalar", lambda e, st=st, yb=yb: e.dma_start(
                    out=g_y2[st * 128:(st + 1) * 128, :], in_=yb[:]),
                   waits=[(c_v, v_ev)], inc=wsy, inc_n=16)
                ln3_g[st] = emit_ln_tile(yb[:], [(c_v, v_ev)],
                                         eng="vector")
            y2_done = [(w_y[0], w_y[0].n), (w_y[1], w_y[1].n)]
            p7_end_t = c_t.n

        if "p8" in phases:
            p8a_end = emit_gate_up("f2", [], [(c_t, p7_end_t)])
            emit_down("f2", g_y2, y_out, w_out, y2_done, p8a_end,
                      ({0: ln3_g[ST - 2], 1: ln3_g[ST - 1]}
                       if ln3_g else {}))
            wait("sync", (w_out[0], w_out[0].n), (w_out[1], w_out[1].n))

        drains = [w_xln, w_h, w_y, w_qk, w_v, w_cx, w_out, d_x]
        for sp in drains:
            wait("sync", (sp[0], sp[0].n), (sp[1], sp[1].n))
        for s1 in (w_wm, w_wb, w_cst, d_lut):
            wait("sync", (s1, s1.n))

        blk = ctx.enter_context(nc.Block())
        for eng in ENGS:
            def runner(e, eng=eng):
                for fn in streams[eng]:
                    fn(e)
            getattr(blk, eng)(runner)

    return nc


def kernel(**inputs):
    x = np.ascontiguousarray(np.asarray(inputs["L0_attn_hidden_states"],
                                        dtype=np.float32))
    wqT = np.ascontiguousarray(np.asarray(inputs["L0_attn_Wq"],
                                          dtype=np.float32).T)
    wkT = np.ascontiguousarray(np.asarray(inputs["L0_attn_Wk"],
                                          dtype=np.float32).T)
    wvT = np.ascontiguousarray(np.asarray(inputs["L0_attn_Wv"],
                                          dtype=np.float32).T)
    woT = np.ascontiguousarray(np.asarray(inputs["L0_attn_Wo"],
                                          dtype=np.float32).T)
    lut = np.ascontiguousarray(np.asarray(inputs["L0_ffn_lut"],
                                          dtype=np.float32).reshape(1, 16))
    gidxT = np.asarray(inputs["L0_ffn_gate_idx"], dtype=np.int32).T
    uidxT = np.asarray(inputs["L0_ffn_up_idx"], dtype=np.int32).T
    didxT = np.asarray(inputs["L0_ffn_down_idx"], dtype=np.int32).T
    dpad = np.zeros((6144, H), np.int32)
    dpad[0:O] = didxT

    nc = build_program()
    in_maps = [{
        "x": np.ascontiguousarray(x[b]),
        "wqs": np.ascontiguousarray(wqT[b * 256:(b + 1) * 256]),
        "wks": np.ascontiguousarray(wkT[b * 256:(b + 1) * 256]),
        "wvs": np.ascontiguousarray(wvT[b * 256:(b + 1) * 256]),
        "wos": np.ascontiguousarray(woT[b * 256:(b + 1) * 256]),
        "lut": lut,
        "guidxT": np.ascontiguousarray(
            gidxT[b * 512:(b + 1) * 512] if b < 4
            else uidxT[(b - 4) * 512:(b - 3) * 512]),
        "didxTs": np.ascontiguousarray(dpad[b * 768:(b + 1) * 768]),
    } for b in range(8)]
    res = run_bass_kernel_spmd(nc, in_maps, core_ids=list(range(8)))
    return np.stack([res.results[b]["y"] for b in range(8)], axis=0)


# version 34
# speedup vs baseline: 1.0378x; 1.0378x over previous
import numpy as np
import contextlib
import math
import concourse.bass as bass
import concourse.mybir as mybir
from concourse.bass_utils import run_bass_kernel_spmd

F32 = mybir.dt.float32
BF16 = mybir.dt.bfloat16
I32 = mybir.dt.int32
AF = mybir.ActivationFunctionType
ALU = mybir.AluOpType
f32 = np.float32

S = 4096
H = 2048
O = 5504
OC = O // 128
HC = H // 128
NH = 16
ST = S // 128
TWO_PI = float(2 * np.pi)
C1 = float(np.float32(2 * np.pi))
C2 = float(np.float64(2 * np.pi) - np.float64(np.float32(2 * np.pi)))
LN1E4 = float(np.log(10000.0))
EXP_SCALE = 1.0 / math.sqrt(128.0)
EXP_BIAS = -8.0
NEG_MASK = -30000.0

DEBUG_TAPS = ()
P6_SERIAL = False
P1_ILV = True
LN1_ENG = "vector"
P6_NQC = 8

ENGS = ("sync", "scalar", "vector", "tensor", "gpsimd")
ALL_PHASES = ("p0", "p1", "p2", "p3", "p4", "p5", "p6", "p7", "p8")


def build_program(phases=ALL_PHASES):
    nc = bass.Bass("TRN2", target_bir_lowering=False)

    x_in = nc.dram_tensor("x", [S, H], F32, kind="ExternalInput").ap()
    wqs_in = nc.dram_tensor("wqs", [256, H], F32, kind="ExternalInput").ap()
    wks_in = nc.dram_tensor("wks", [256, H], F32, kind="ExternalInput").ap()
    wvs_in = nc.dram_tensor("wvs", [256, H], F32, kind="ExternalInput").ap()
    wos_in = nc.dram_tensor("wos", [256, H], F32, kind="ExternalInput").ap()
    lut_in = nc.dram_tensor("lut", [1, 16], F32, kind="ExternalInput").ap()
    guidxT_in = nc.dram_tensor("guidxT", [512, O], I32,
                               kind="ExternalInput").ap()
    didxT_in = nc.dram_tensor("didxTs", [768, H], I32,
                              kind="ExternalInput").ap()
    y_out = nc.dram_tensor("y", [S, H], F32, kind="ExternalOutput").ap()

    def scr(name, shape, dtype):
        kind = "ExternalOutput" if name in DEBUG_TAPS else "Internal"
        return nc.dram_tensor(name, shape, dtype, kind=kind).ap()

    PIECES = [(0, 256), (256, 256), (512, 512), (1024, 1024),
              (2048, 1024), (3072, 1024), (4096, 1024), (5120, 384)]
    NGU = len(PIECES)
    g_sh = [scr(f"g_sh{b}", [512, w], BF16) for b, (c0, w) in
            enumerate(PIECES)]
    g_gup = [nc.dram_tensor(f"g_gup{b}", [4096, w], BF16,
                            kind="Internal", addr_space="Shared").ap()
             for b, (c0, w) in enumerate(PIECES)]
    g_shd = scr("g_shd", [768, H], BF16)
    g_wdp = nc.dram_tensor("g_wdp", [6144, H], BF16,
                           kind="Internal", addr_space="Shared").ap()
    g_bar_i = scr("g_bar_i", [1, 64], I32)
    g_bar_o = scr("g_bar_o", [1, 64], I32)
    g_h = scr("g_h", [OC * 128, S], BF16)
    g_y1 = scr("g_y1", [S, H], F32)
    g_y2 = scr("g_y2", [S, H], F32)
    g_xln = scr("g_xln", [S, H], BF16)
    g_wq_s = scr("g_wq_s", [256, H], BF16)
    g_wk_s = scr("g_wk_s", [256, H], BF16)
    g_wv_s = scr("g_wv_s", [256, H], BF16)
    g_wo_s = scr("g_wo_s", [256, H], BF16)
    g_wq = nc.dram_tensor("g_wq", [H, H], BF16, kind="Internal",
                          addr_space="Shared").ap()
    g_wk = nc.dram_tensor("g_wk", [H, H], BF16, kind="Internal",
                          addr_space="Shared").ap()
    g_wv = nc.dram_tensor("g_wv", [H, H], BF16, kind="Internal",
                          addr_space="Shared").ap()
    g_wo = nc.dram_tensor("g_wo", [H, H], BF16, kind="Internal",
                          addr_space="Shared").ap()
    g_qT = scr("g_qT", [NH * 128, S], BF16)
    g_kT = scr("g_kT", [NH * 128, S], BF16)
    g_v = scr("g_v", [S, H], BF16)
    g_ctxT = scr("g_ctxT", [NH * 128, S], BF16)
    g_dbg_p = scr("g_dbg_p", [128, 2048], BF16)
    g_dbg_z = scr("g_dbg_z", [128, 512], F32)
    g_dbg_q = scr("g_dbg_q", [128, S], BF16)
    g_dbg_k = scr("g_dbg_k", [128, S], BF16)

    ctx = contextlib.ExitStack()
    with ctx:
        def sbt(name, shape, dtype, **kw):
            return ctx.enter_context(nc.sbuf_tensor(name, shape, dtype, **kw))

        arena = sbt("arena", [128, 51200], BF16, is_transpose_dest=True)
        ar_f32 = arena.bitcast(F32)
        ar_i32 = arena.bitcast(I32)
        wcol = [sbt(f"wcol{i}", [128, 2 * HC * 128], BF16) for i in range(2)]
        hbuf = [sbt(f"hbuf{i}", [128, 2048], BF16) for i in range(2)]
        idxb = [sbt(f"idxb{i}", [128, 1024], I32) for i in range(2)]
        mtmp = sbt("mtmp", [128, 1024], BF16)
        maccs = [sbt(f"macc{i}", [128, 1024], BF16) for i in range(2)]
        xtile = [sbt(f"xtile{i}", [128, 2048], F32) for i in range(2)]
        ybuf = [sbt(f"ybuf{i}", [128, 2048], F32) for i in range(2)]
        cosT = sbt("cosT", [128, S], BF16)
        ssinT = sbt("ssinT", [128, S], BF16)
        rtmp = sbt("rtmp", [128, 1024], F32)
        masks = sbt("masks", [128, 4 * 512], BF16)
        ones_bf = sbt("ones_bf", [128, 128], BF16)
        ones_f = sbt("ones_f", [1, 128], F32)
        lutb = sbt("lutb", [128, 16], F32)
        sgb = [sbt(f"sgb{i}", [128, 512], BF16) for i in range(2)]
        stats = sbt("stats", [128, 16], F32)
        qkbuf = [sbt(f"qkbuf{i}", [128, 2048], BF16) for i in range(2)]
        junk = arena[:, 49152:51200]

        psb = [ctx.enter_context(nc.psum_tensor(f"pb{i}", [128, 512], F32))
               for i in range(8)]

        class Sem:
            __slots__ = ("h", "n")
            _k = [0]

            def __init__(self, name):
                Sem._k[0] += 1
                self.h = ctx.enter_context(
                    nc.semaphore(f"{name}{Sem._k[0]}"))
                self.n = 0

        streams = {e: [] for e in ENGS}

        def op(eng, fn, waits=(), inc=None, inc_n=1):
            if inc is not None:
                inc.n += inc_n
            ws = tuple((s.h, v) for s, v in waits if v > 0)
            tgt = (inc.h, inc_n) if inc is not None else None

            def closure(e, fn=fn, ws=ws, tgt=tgt):
                for h, v in ws:
                    e.wait_ge(h, v)
                inst = fn(e)
                if tgt is not None:
                    inst.then_inc(*tgt)
            streams[eng].append(closure)

        def wait(eng, *waits):
            ws = tuple((s.h, v) for s, v in waits if v > 0)

            def closure(e, ws=ws):
                for h, v in ws:
                    e.wait_ge(h, v)
            streams[eng].append(closure)

        def spair(name):
            return (Sem(name), Sem(name))

        c_v, c_s, c_t, c_g = Sem("cv"), Sem("cs"), Sem("ct"), Sem("cg")

        d_lut = Sem("dlut")
        lutb_ready = 0
        tables_done_s = 0
        masks_done_g = 0
        def emit_mask_consts():
            op("gpsimd", lambda e: e.memset(ones_bf[:], 1.0), inc=c_g)
            op("gpsimd", lambda e: e.memset(masks[:], 1.0), inc=c_g)
            for j in range(4):
                op("gpsimd",
                   lambda e, j=j: e.affine_select(
                       masks[:, j * 512:(j + 1) * 512],
                       masks[:, j * 512:(j + 1) * 512],
                       [[1, 512]], ALU.is_ge, 0.0,
                       base=-128 * j, channel_multiplier=-1),
                   inc=c_g)
            return c_g.n

        if "p0" in phases:
            op("gpsimd", lambda e: e.memset(ones_f[:], 1.0), inc=c_g)
            g_consts = c_g.n
            op("sync", lambda e: e.dma_start(out=lutb[0:1, :], in_=lut_in),
               inc=d_lut, inc_n=16)
            op("tensor",
               lambda e: e.matmul(psb[7][:, 0:16], ones_f[:], lutb[0:1, :],
                                  start=True, stop=True),
               waits=[(d_lut, 16), (c_g, g_consts)], inc=c_t)
            op("scalar", lambda e: e.copy(lutb[:], psb[7][:, 0:16]),
               waits=[(c_t, c_t.n)], inc=c_s)
            lutb_ready = c_s.n

            tmpf = ar_f32[:, 0:4096]
            tmpf2 = ar_f32[:, 4096:8192]
            ki = ar_i32[:, 8192:12288]
            kf = ar_f32[:, 12288:16384]
            sinf = ar_f32[:, 16384:20480]
            op("gpsimd",
               lambda e: e.iota(stats[:, 7:8], [[0, 1]], base=0,
                                channel_multiplier=1,
                                allow_small_or_imprecise_dtypes=True),
               inc=c_g)
            op("gpsimd",
               lambda e: e.tensor_scalar(stats[:, 6:7], stats[:, 7:8], 64.0,
                                         64.0, ALU.is_ge, ALU.mult))
            op("gpsimd",
               lambda e: e.tensor_tensor(stats[:, 7:8], stats[:, 7:8],
                                         stats[:, 6:7], ALU.subtract),
               inc=c_g)
            g_rr = c_g.n
            op("scalar",
               lambda e: e.activation(stats[:, 6:7], stats[:, 7:8], AF.Exp,
                                      scale=f32(-LN1E4 / 64.0)),
               waits=[(c_g, g_rr)], inc=c_s)
            s_invf = c_s.n
            op("gpsimd",
               lambda e: e.iota(tmpf, [[1, 4096]], base=0,
                                channel_multiplier=0,
                                allow_small_or_imprecise_dtypes=True),
               inc=c_g)
            g_t = c_g.n

            def sin_table(dst, shift):
                w0 = [(c_g, g_t), (c_s, max(s_invf, c_s.n))]
                if shift == 0.0:
                    op("vector",
                       lambda e: e.tensor_single_scalar(
                           tmpf2, tmpf, stats[:, 6:7], ALU.mult),
                       waits=w0)
                else:
                    op("vector",
                       lambda e, sh=shift: e.tensor_scalar(
                           tmpf2, tmpf, stats[:, 6:7], sh, ALU.mult, ALU.add),
                       waits=w0)
                op("vector", lambda e: e.tensor_single_scalar(
                    sinf, tmpf2, 1.0 / TWO_PI, ALU.mult))
                op("vector", lambda e: e.tensor_copy(ki, sinf))
                op("vector", lambda e: e.tensor_copy(kf, ki))
                op("vector", lambda e: e.scalar_tensor_tensor(
                    sinf, kf, -C1, tmpf2, ALU.mult, ALU.add))
                op("vector", lambda e: e.scalar_tensor_tensor(
                    sinf, kf, -C2, sinf, ALU.mult, ALU.add), inc=c_v)
                vr = c_v.n
                op("scalar", lambda e, d=dst: e.activation(d, sinf, AF.Sin),
                   waits=[(c_v, vr)], inc=c_s)
                return c_s.n

            sin_table(cosT[:], math.pi / 2)
            s_tab = sin_table(ssinT[:], 0.0)
            tables_done_s = s_tab
            op("vector",
               lambda e: e.tensor_single_scalar(ssinT[0:64, :],
                                                ssinT[0:64, :],
                                                -1.0, ALU.mult),
               waits=[(c_s, s_tab)], inc=c_v)
            tables_done_v = c_v.n
        else:
            tables_done_v = 0

        CH = 1024
        mat_jobs = []
        piece_last_job = []
        for b, (c0, w) in enumerate(PIECES):
            for t in range(4):
                mat_jobs.append(
                    (guidxT_in[t * 128:(t + 1) * 128, c0:c0 + w],
                     g_sh[b][t * 128:(t + 1) * 128, :], w))
            piece_last_job.append(len(mat_jobs) - 1)
        for t in range(6):
            for c0 in (0, 1024):
                mat_jobs.append(
                    (didxT_in[t * 128:(t + 1) * 128, c0:c0 + CH],
                     g_shd[t * 128:(t + 1) * 128, c0:c0 + CH], CH))
        piece_last_job.append(len(mat_jobs) - 1)

        d_idx = spair("didx")
        w_wm = Sem("wwm")
        w_cst = Sem("wcst")
        cc_sem = Sem("ccw")

        if "p0" in phases:
            for w_in, w_sh in ((wqs_in, g_wq_s), (wks_in, g_wk_s),
                               (wvs_in, g_wv_s), (wos_in, g_wo_s)):
                op("gpsimd", lambda e, a=w_in, b=w_sh: e.dma_start(
                    out=b, in_=a), inc=w_cst, inc_n=16)
            for piece in range(NGU + 1):
                src = g_sh[piece].opt() if piece < NGU else g_shd.opt()
                dst = g_gup[piece].opt() if piece < NGU else g_wdp.opt()
                op("gpsimd", lambda e, src=src, dst=dst: e.collective_compute(
                    "AllGather", ALU.bypass,
                    replica_groups=[[0, 1, 2, 3, 4, 5, 6, 7]],
                    ins=[src], outs=[dst]),
                   waits=[(w_wm, 16 * (piece_last_job[piece] + 1))],
                   inc=cc_sem)
            for w_sh, w_full in ((g_wq_s, g_wq), (g_wk_s, g_wk),
                                 (g_wv_s, g_wv), (g_wo_s, g_wo)):
                op("gpsimd", lambda e, a=w_sh, b=w_full: e.collective_compute(
                    "AllGather", ALU.bypass,
                    replica_groups=[[0, 1, 2, 3, 4, 5, 6, 7]],
                    ins=[a.opt()], outs=[b.opt()]),
                   waits=[(w_cst, 64)], inc=cc_sem)
            op("gpsimd", lambda e: e.collective_compute(
                "AllReduce", ALU.add,
                replica_groups=[[0, 1, 2, 3, 4, 5, 6, 7]],
                ins=[g_bar_i.opt()], outs=[g_bar_o.opt()]),
               inc=cc_sem)
        cc_all = cc_sem.n

        mat_state = {"k": 0, "j": 0, "read_done": {}}
        NMJ = len(mat_jobs) if "p0" in phases else 0

        def emit_idx_load(k):
            in_ap, _, w = mat_jobs[k]
            rd = mat_state["read_done"].get(k - 2, 0)
            op("scalar", lambda e, a=in_ap, b=idxb[k % 2], w=w:
               e.dma_start(out=b[:, 0:w], in_=a),
               waits=[(c_v, rd)], inc=d_idx[k % 2], inc_n=16)

        def pump(n):
            for _ in range(n):
                k = mat_state["k"]
                if k >= NMJ:
                    return
                j = mat_state["j"]
                in_ap, out_ap, w = mat_jobs[k]
                buf = idxb[k % 2]
                mc = maccs[k % 2]
                if j == 0:
                    op("vector", lambda e, buf=buf, mc=mc, w=w:
                       e.tensor_scalar(mc[:, 0:w], buf[:, 0:w], 0.0,
                                       lutb[:, 0:1], ALU.is_equal, ALU.mult),
                       waits=[(d_idx[k % 2], 16 * (k // 2 + 1)),
                              (c_s, lutb_ready),
                              (w_wm, 16 * max(0, k - 1))])
                elif j % 2 == 1:
                    jj = (j + 1) // 2
                    op("vector", lambda e, buf=buf, jj=jj, w=w:
                       e.tensor_scalar(mtmp[:, 0:w], buf[:, 0:w], float(jj),
                                       lutb[:, jj:jj + 1],
                                       ALU.is_equal, ALU.mult),
                       inc=(c_v if j == 29 else None))
                    if j == 29:
                        mat_state["read_done"][k] = c_v.n
                else:
                    op("vector", lambda e, mc=mc, w=w: e.tensor_tensor(
                        mc[:, 0:w], mc[:, 0:w], mtmp[:, 0:w], ALU.add),
                       inc=(c_v if j == 30 else None))
                if j == 30:
                    op("scalar", lambda e, o=out_ap, mc=mc, w=w: e.dma_start(
                        out=o, in_=mc[:, 0:w]),
                       waits=[(c_v, c_v.n)], inc=w_wm, inc_n=16)
                    if k + 2 < NMJ:
                        emit_idx_load(k + 2)
                    mat_state["k"] = k + 1
                    mat_state["j"] = 0
                else:
                    mat_state["j"] = j + 1

        w_xln = spair("wxln")
        ln_state = {"i": 0, "tile_store": {}}

        def ln_a(i, src_ap, src_waits):
            cb = (i % 2) * 8
            op("scalar", lambda e, a=src_ap, cb=cb: e.activation(
                junk[:], a, AF.Copy, accum_out=stats[:, cb:cb + 1]),
               waits=src_waits, inc=c_s)
            op("scalar", lambda e, a=src_ap, cb=cb: e.activation(
                junk[:], a, AF.Square, accum_out=stats[:, cb + 1:cb + 2]),
               inc=c_s)
            op("scalar", lambda e, cb=cb: e.mul(
                stats[:, cb + 2:cb + 3], stats[:, cb:cb + 1], 1.0 / H))
            op("scalar", lambda e, cb=cb: e.mul(
                stats[:, cb + 3:cb + 4], stats[:, cb + 1:cb + 2], 1.0 / H),
               inc=c_s)
            return c_s.n

        def ln_b(i, src_ap, s_stats, eng="gpsimd"):
            cx = c_g if eng == "gpsimd" else c_v
            cb = (i % 2) * 8
            op(eng, lambda e, cb=cb: e.tensor_tensor(
                stats[:, cb + 7:cb + 8], stats[:, cb + 2:cb + 3],
                stats[:, cb + 2:cb + 3], ALU.mult),
               waits=[(c_s, s_stats)], inc=cx)
            wait(eng, (cx, cx.n))
            op(eng, lambda e, cb=cb: e.tensor_tensor(
                stats[:, cb + 3:cb + 4], stats[:, cb + 3:cb + 4],
                stats[:, cb + 7:cb + 8], ALU.subtract), inc=cx)
            wait(eng, (cx, cx.n))
            op(eng, lambda e, cb=cb: e.tensor_single_scalar(
                stats[:, cb + 5:cb + 6], stats[:, cb + 2:cb + 3],
                -1.0, ALU.mult), inc=cx)
            g_st = cx.n
            op("scalar", lambda e, cb=cb: e.activation(
                stats[:, cb + 4:cb + 5], stats[:, cb + 3:cb + 4], AF.Ln,
                bias=f32(1e-5)),
               waits=[(cx, g_st)])
            op("scalar", lambda e, cb=cb: e.activation(
                stats[:, cb + 4:cb + 5], stats[:, cb + 4:cb + 5], AF.Exp,
                scale=f32(-0.5)),
               inc=c_s)
            s_rstd = c_s.n
            dst = hbuf[i % 2]
            ws = w_xln[i % 2]
            op(eng, lambda e, a=src_ap, d=dst, cb=cb: e.tensor_scalar(
                d[:], a, stats[:, cb + 5:cb + 6], stats[:, cb + 4:cb + 5],
                ALU.add, ALU.mult),
               waits=[(c_s, s_rstd), (ws, ws.n)], inc=cx)
            g_norm = cx.n
            row = (i % ST) * 128
            op("scalar", lambda e, r=row, d=dst: e.dma_start(
                out=g_xln[r:r + 128, :], in_=d[:]),
               waits=[(cx, g_norm)], inc=ws, inc_n=16)
            ln_state["tile_store"][i % ST] = (ws, ws.n)
            return (cx, cx.n)

        def emit_ln_tile(src_ap, src_waits, eng="gpsimd"):
            i = ln_state["i"]
            ln_state["i"] = i + 1
            s_stats = ln_a(i, src_ap, src_waits)
            return ln_b(i, src_ap, s_stats, eng)

        d_x = spair("dx")
        w_wb = Sem("wwb")
        if "p1" in phases:
            if NMJ:
                emit_idx_load(0)
                emit_idx_load(1)
            x_rd = {}
            pend = {}
            for i in range(ST):
                ld = d_x[i % 2]
                buf = xtile[i % 2]
                w = x_rd.get(i - 2)
                op("sync" if i < 16 else "scalar",
                   lambda e, i=i, b=buf: e.dma_start(
                       out=b[:], in_=x_in[i * 128:(i + 1) * 128, :]),
                   waits=[w] if w else [], inc=ld, inc_n=16)
                ln_state["i"] = i + 1
                pend[i] = (buf, ln_a(i, buf[:], [(ld, ld.n)]))
                if i >= 1:
                    b1, s1 = pend.pop(i - 1)
                    x_rd[i - 1] = ln_b(i - 1, b1[:], s1, eng=LN1_ENG)
                pump(31 if mat_state["k"] < 12 else
                     (12 if i < 16 else 31))
            b1, s1 = pend.pop(ST - 1)
            x_rd[ST - 1] = ln_b(ST - 1, b1[:], s1, eng=LN1_ENG)

        if "p0" in phases:
            masks_done_g = emit_mask_consts()
        if "t_gup" in DEBUG_TAPS and "p0" in phases:
            t_gup = [nc.dram_tensor(f"t_gup{b}", [4096, w], BF16,
                                    kind="ExternalOutput").ap()
                     for b, (c0, w) in enumerate(PIECES)]
            t_wdp = nc.dram_tensor("t_wdp", [6144, H], BF16,
                                   kind="ExternalOutput").ap()
            for b in range(NGU):
                op("scalar", lambda e, b=b: e.dma_start(
                    out=t_gup[b], in_=g_gup[b]),
                   waits=[(cc_sem, b + 1)], inc=w_wb, inc_n=16)
            op("scalar", lambda e: e.dma_start(out=t_wdp, in_=g_wdp),
               waits=[(cc_sem, 7)], inc=w_wb, inc_n=16)

        d_xlt = Sem("dxlt")
        d_wc = spair("dwc")
        w_h = spair("wh")

        def emit_gate_up(tag, side_jobs, arena_free):
            xlt = arena[:, 0:HC * 2048].rearrange("p (hc s) -> p hc s", hc=HC)
            ts = dict(ln_state["tile_store"])
            xln_half = [[ts[i] for i in (14, 15) if i in ts],
                        [ts[i] for i in (30, 31) if i in ts]]
            st_wcol, st_evac, st_hst = {}, {}, {}
            for sh in range(2):
                ct_now = c_t.n
                if sh == 0:
                    ws = xln_half[0] + list(arena_free)
                else:
                    ws = xln_half[1] + [(c_t, ct_now)]
                op("sync",
                   lambda e, sh=sh: e.dma_start_transpose(
                       xlt[:, :, :],
                       g_xln[sh * 2048:(sh + 1) * 2048, :]),
                   waits=ws, inc=d_xlt, inc_n=16)
                xlt_done = d_xlt.n
                for oc in range(OC):
                    k = sh * OC + oc
                    wsem = d_wc[k % 2]
                    wb = wcol[k % 2]
                    cb = next(p for p, (c0, pw) in enumerate(PIECES)
                              if c0 <= oc * 128 < c0 + pw)
                    co = oc * 128 - PIECES[cb][0]
                    blk_ws = [(cc_sem, cb + 1)] if sh == 0 else []
                    op("sync", lambda e, cb=cb, co=co, wb=wb: e.dma_start(
                        out=wb[:, 0:HC * 128].rearrange(
                            "p (hc o) -> p hc o", hc=HC),
                        in_=g_gup[cb][0:2048].rearrange(
                            "(hc p) o -> p hc o", p=128)[:, :, co:co + 128]),
                       waits=blk_ws + [(c_t, st_wcol.get(k - 2, 0))],
                       inc=wsem, inc_n=16)
                    op("sync", lambda e, cb=cb, co=co, wb=wb: e.dma_start(
                        out=wb[:, HC * 128:2 * HC * 128].rearrange(
                            "p (hc o) -> p hc o", hc=HC),
                        in_=g_gup[cb][2048:4096].rearrange(
                            "(hc p) o -> p hc o", p=128)[:, :, co:co + 128]),
                       inc=wsem, inc_n=16)
                    w_ready = wsem.n
                    for sc in range(4):
                        slot = k * 4 + sc
                        pg = psb[slot % 4]
                        pu = psb[4 + slot % 4]
                        mw = [(wsem, w_ready),
                              (c_v, st_evac.get(slot - 4, 0))]
                        if slot == 3:
                            mw.append((c_s, lutb_ready))
                        if oc == 0 and sc == 0:
                            mw.append((d_xlt, xlt_done))

                        def mm(e, xlt=xlt, wb=wb, pg=pg, pu=pu, sc=sc):
                            for h2 in range(HC):
                                e.matmul(pg[:], wb[:, h2 * 128:(h2 + 1) * 128],
                                         xlt[:, h2, sc * 512:(sc + 1) * 512],
                                         start=(h2 == 0), stop=(h2 == HC - 1))
                            r = None
                            for h2 in range(HC):
                                r = e.matmul(
                                    pu[:],
                                    wb[:, (HC + h2) * 128:(HC + h2 + 1) * 128],
                                    xlt[:, h2, sc * 512:(sc + 1) * 512],
                                    start=(h2 == 0), stop=(h2 == HC - 1))
                            return r
                        op("tensor", mm, waits=mw, inc=c_t)
                        t_mm = c_t.n
                        st_wcol[k] = t_mm
                        sg = sgb[slot % 2]
                        op("scalar", lambda e, pg=pg, sg=sg: e.activation(
                            sg[:], pg[:], AF.Silu),
                           waits=[(c_t, t_mm),
                                  (c_v, st_evac.get(slot - 2, 0))],
                           inc=c_s)
                        s_silu = c_s.n
                        hb = hbuf[k % 2]
                        hw = w_h[k % 2]
                        vw = [(c_s, s_silu)]
                        if sc == 0:
                            vw.append((hw, hw.n))
                            if k < 2:
                                vw.append((w_xln[0], w_xln[0].n))
                                vw.append((w_xln[1], w_xln[1].n))
                        op("vector",
                           lambda e, hb=hb, sg=sg, pu=pu, sc=sc:
                           e.tensor_tensor(hb[:, sc * 512:(sc + 1) * 512],
                                           sg[:], pu[:], ALU.mult),
                           waits=vw, inc=c_v)
                        st_evac[slot] = c_v.n
                        pump(8)
                    v_h = c_v.n
                    op("scalar", lambda e, oc=oc, sh=sh, hb=hbuf[k % 2]:
                       e.dma_start(
                           out=g_h[oc * 128:(oc + 1) * 128,
                                   sh * 2048:(sh + 1) * 2048],
                           in_=hb[:]),
                       waits=[(c_v, v_h)], inc=hw, inc_n=16)
                    st_hst[k] = hw.n
            return c_t.n

        d_wd = Sem("dwd")
        d_hc = spair("dhc")
        d_xq = spair("dxq")
        w_y = spair("wy")
        w_out = spair("wout")

        down_store = {}

        def emit_down(tag, res_dram, out_dram, wsems, pre_waits, arena_free_t,
                      ybuf_war, tile_cb=None):
            wd_sb = arena[:, 0:OC * 512].rearrange("p (oc d) -> p oc d", oc=OC)
            hb0 = OC * 512
            hcols = [arena[:, hb0 + i * O: hb0 + (i + 1) * O]
                     .rearrange("p (oc s) -> p oc s", oc=OC) for i in range(2)]
            st_hcol, st_evac, st_xq = {}, {}, {}
            wd_prev_used = arena_free_t
            h_now = [(w_h[0], w_h[0].n), (w_h[1], w_h[1].n)]
            OCS = [(0, 11), (11, 11), (22, 11), (33, 10)]
            for dq in range(4):
                wd_subs = []
                for sj, (o0, on) in enumerate(OCS):
                    op("sync", lambda e, dq=dq, o0=o0, on=on: e.dma_start(
                        out=wd_sb[:, o0:o0 + on, :],
                        in_=g_wdp.rearrange("(oc p) d -> p oc d", p=128)[
                            :, o0:o0 + on, dq * 512:(dq + 1) * 512]),
                       waits=((list(pre_waits) + h_now if dq == 0 else []) +
                              [(c_t, wd_prev_used)]) if sj == 0 else [],
                       inc=d_wd, inc_n=16)
                    wd_subs.append(d_wd.n)
                wd_ready = d_wd.n
                for st in range(ST):
                    k = dq * ST + st
                    hsem = d_hc[k % 2]
                    op("sync", lambda e, st=st, hc=hcols[k % 2]: e.dma_start(
                        out=hc,
                        in_=g_h.rearrange("(oc p) s -> p oc s", p=128)[
                            :, :, st * 128:(st + 1) * 128]),
                       waits=[(c_t, st_hcol.get(
                           k - 2, arena_free_t if k < 2 else 0))],
                       inc=hsem, inc_n=16)
                    h_ready = hsem.n
                    xq = d_xq[k % 2]
                    xb = xtile[k % 2]
                    op("sync", lambda e, st=st, dq=dq, xb=xb: e.dma_start(
                        out=xb[:, 0:512],
                        in_=res_dram[st * 128:(st + 1) * 128,
                                     dq * 512:(dq + 1) * 512]),
                       waits=[(c_v, st_xq.get(k - 2, 0))], inc=xq, inc_n=16)
                    x_ready = xq.n
                    ps = psb[4 + (k % 2)]

                    subw = ([(o0, wd_subs[sj]) for sj, (o0, on) in
                             enumerate(OCS)] if st == 0 else [])

                    def mm(e, hc=hcols[k % 2], ps=ps, subw=subw):
                        r = None
                        bnd = dict(subw)
                        for oc in range(OC):
                            if oc in bnd:
                                e.wait_ge(d_wd.h, bnd[oc])
                            r = e.matmul(ps[:], hc[:, oc, :], wd_sb[:, oc, :],
                                         start=(oc == 0), stop=(oc == OC - 1))
                        return r
                    op("tensor", mm,
                       waits=([(d_wd, wd_ready)] if st > 0 else []) +
                             [(hsem, h_ready),
                              (c_v, st_evac.get(k - 2, 0))],
                       inc=c_t)
                    t_mm = c_t.n
                    st_hcol[k] = t_mm
                    if st == ST - 1:
                        wd_prev_used = t_mm
                    yb = ybuf[st % 2]
                    ysem = wsems[k % 2]
                    vw = [(c_t, t_mm), (xq, x_ready), (ysem, ysem.n)]
                    if k < 2 and (k % 2) in ybuf_war:
                        vw.append(ybuf_war[k % 2])
                    op("vector", lambda e, ps=ps, xb=xb, yb=yb, dq=dq:
                       e.scalar_tensor_tensor(
                           yb[:, dq * 512:(dq + 1) * 512], ps[:], 0.5,
                           xb[:, 0:512], ALU.mult, ALU.add),
                       waits=vw, inc=c_v)
                    st_evac[k] = c_v.n
                    st_xq[k] = c_v.n
                    v_ev = c_v.n
                    op("scalar", lambda e, st=st, dq=dq, yb=yb: e.dma_start(
                        out=out_dram[st * 128:(st + 1) * 128,
                                     dq * 512:(dq + 1) * 512],
                        in_=yb[:, dq * 512:(dq + 1) * 512]),
                       waits=[(c_v, v_ev)], inc=ysem, inc_n=16)
                    down_store[(tag, dq, st)] = (ysem, ysem.n)
                    if tile_cb is not None and dq == 3:
                        tile_cb(st)
            return c_t.n

        d_yr = spair("dyr")
        ln2_state = {"y1_rd": {}, "pend": {}}

        def ln2_tile(i):
            ld = d_yr[i % 2]
            y4buf = [ar_f32[:, 16512:18560], ar_f32[:, 18560:20608]]
            buf = y4buf[i % 2]
            y1_rd = ln2_state["y1_rd"]
            pend4 = ln2_state["pend"]
            op("gpsimd", lambda e, i=i, b=buf: e.dma_start(
                out=b, in_=g_y1[i * 128:(i + 1) * 128, :]),
               waits=[down_store[("f1", 3, i)]] +
                     ([y1_rd[i - 2]] if i - 2 in y1_rd else []),
               inc=ld, inc_n=16)
            ln_state["i"] += 1
            li = ln_state["i"] - 1
            pend4[i] = (li, buf, ln_a(li, buf, [(ld, ld.n)]))
            if i >= 1:
                li1, b1, s1 = pend4.pop(i - 1)
                y1_rd[i - 1] = ln_b(li1, b1, s1, eng="vector")
            if i == ST - 1:
                li1, b1, s1 = pend4.pop(ST - 1)
                y1_rd[ST - 1] = ln_b(li1, b1, s1, eng="vector")

        p2_end = 0
        if "p2" in phases:
            p2_end = emit_gate_up(
                "f1", [], [(c_s, tables_done_s), (c_v, tables_done_v)])
        while mat_state["k"] < NMJ:
            pump(31)
        p3_end = p2_end
        if "p3" in phases:
            p3_end = emit_down("f1", x_in, g_y1, w_y,
                               [(cc_sem, min(NGU + 1, cc_all))], p2_end, {},
                               tile_cb=(ln2_tile if "p4" in phases
                                        else None))
            y1_done = [(w_y[0], w_y[0].n), (w_y[1], w_y[1].n)]



        d_wv = Sem("dwv")
        w_qk = spair("wqk")
        w_v = spair("wv")
        p5_end_t = p3_end
        if "p5" in phases:
            xlt = arena[:, 0:HC * 2048].rearrange("p (hc s) -> p hc s", hc=HC)
            wvslabs = [arena[:, 32768 + i * 8192:32768 + (i + 1) * 8192]
                       .rearrange("p (hc d) -> p hc d", hc=HC)
                       for i in range(2)]
            st_wcol, st_rope = {}, {}
            vev_mm, vev_ev = {}, {}
            qk_war = {}
            ts5 = dict(ln_state["tile_store"])
            xln_half5 = [[ts5[i] for i in (14, 15) if i in ts5],
                         [ts5[i] for i in (30, 31) if i in ts5]]
            wb0 = wcol[0]
            op("sync", lambda e, wb=wb0: e.dma_start(
                out=wb[:, 0:HC * 128].rearrange("p (hc d) -> p hc d", hc=HC),
                in_=g_wq.rearrange("(hc p) d -> p hc d", p=128)[:, :, 0:128]),
               waits=[(c_t, p2_end), (cc_sem, cc_all)],
               inc=d_wc[0], inc_n=16)
            op("sync", lambda e, wb=wb0: e.dma_start(
                out=wb[:, HC * 128:2 * HC * 128].rearrange(
                    "p (hc d) -> p hc d", hc=HC),
                in_=g_wk.rearrange("(hc p) d -> p hc d", p=128)[:, :, 0:128]),
               inc=d_wc[0], inc_n=16)
            prefetched = {"done": False}
            for sh in range(2):
                ct_now5 = c_t.n
                if sh == 0:
                    ws = xln_half5[0] + [(c_t, p3_end)]
                else:
                    ws = xln_half5[1] + [(c_t, ct_now5)]
                op("sync",
                   lambda e, sh=sh: e.dma_start_transpose(
                       xlt[:, :, :],
                       g_xln[sh * 2048:(sh + 1) * 2048, :]),
                   waits=ws, inc=d_xlt, inc_n=16)
                xlt_done = d_xlt.n
                for dc in range(NH):
                    k = sh * NH + dc
                    wsem = d_wc[k % 2]
                    wb = wcol[k % 2]
                    if k == 0:
                        w_ready = wsem.n
                    else:
                        op("sync", lambda e, dc=dc, wb=wb: e.dma_start(
                            out=wb[:, 0:HC * 128].rearrange(
                                "p (hc d) -> p hc d", hc=HC),
                            in_=g_wq.rearrange("(hc p) d -> p hc d", p=128)[
                                :, :, dc * 128:(dc + 1) * 128]),
                           waits=[(c_t, st_wcol.get(
                               k - 2, p2_end if k < 2 else 0))],
                           inc=wsem, inc_n=16)
                        op("sync", lambda e, dc=dc, wb=wb: e.dma_start(
                            out=wb[:, HC * 128:2 * HC * 128].rearrange(
                                "p (hc d) -> p hc d", hc=HC),
                            in_=g_wk.rearrange("(hc p) d -> p hc d", p=128)[
                                :, :, dc * 128:(dc + 1) * 128]),
                           inc=wsem, inc_n=16)
                        w_ready = wsem.n
                    for qk in range(2):
                        qbuf = qkbuf[qk]
                        wqksem = w_qk[qk]
                        for sc in range(4):
                            slot = (k * 2 + qk) * 4 + sc
                            ps = psb[(slot % 2) + qk * 2]
                            mw = [(wsem, w_ready),
                                  (c_v, st_rope.get((qk, slot - 2), 0))]
                            if dc == 0 and qk == 0 and sc == 0:
                                mw.append((d_xlt, xlt_done))

                            def mm(e, xlt=xlt, wb=wb, ps=ps, sc=sc, qk=qk):
                                r = None
                                for h2 in range(HC):
                                    r = e.matmul(
                                        ps[:],
                                        wb[:, (qk * HC + h2) * 128:
                                           (qk * HC + h2 + 1) * 128],
                                        xlt[:, h2, sc * 512:(sc + 1) * 512],
                                        start=(h2 == 0), stop=(h2 == HC - 1))
                                return r
                            op("tensor", mm, waits=mw, inc=c_t)
                            t_mm = c_t.n
                            st_wcol[k] = t_mm
                            gofs = sh * 2048 + sc * 512
                            t1 = rtmp[:, 0:512]
                            t2 = rtmp[:, 512:1024]
                            vw = [(c_t, t_mm)]
                            if sc == 0:
                                vw.append((wqksem, qk_war.get((qk, k - 2), 0)))
                            if k == 0 and qk == 0 and sc == 0:
                                vw.append((c_v, tables_done_v))
                            op("vector",
                               lambda e, ps=ps, g=gofs: e.tensor_tensor(
                                   t1, ps[:], cosT[:, g:g + 512], ALU.mult),
                               waits=vw)
                            op("vector",
                               lambda e, ps=ps, g=gofs: e.tensor_tensor(
                                   t2[0:64, :], ps[64:128, :],
                                   ssinT[0:64, g:g + 512], ALU.mult))
                            op("vector",
                               lambda e, ps=ps, g=gofs: e.tensor_tensor(
                                   t2[64:128, :], ps[0:64, :],
                                   ssinT[64:128, g:g + 512], ALU.mult))
                            op("vector", lambda e, qbuf=qbuf, sc=sc:
                               e.tensor_tensor(
                                   qbuf[:, sc * 512:(sc + 1) * 512],
                                   t1, t2, ALU.add),
                               inc=c_v)
                            st_rope[(qk, slot)] = c_v.n
                        v_r = c_v.n
                        dstg = g_qT if qk == 0 else g_kT
                        op("scalar",
                           lambda e, d=dstg, dc=dc, sh=sh, qbuf=qbuf:
                           e.dma_start(
                               out=d[dc * 128:(dc + 1) * 128,
                                     sh * 2048:(sh + 1) * 2048],
                               in_=qbuf[:]),
                           waits=[(c_v, v_r)], inc=wqksem, inc_n=16)
                        qk_war[(qk, k)] = wqksem.n
                for dc4 in range(4):
                    kv = sh * 4 + dc4
                    wvslab = wvslabs[kv % 2]
                    wv_extra = ([(w_xln[0], w_xln[0].n),
                                 (w_xln[1], w_xln[1].n),
                                 (c_g, c_g.n), (c_v, c_v.n)]
                                if kv < 2 else [])
                    op("sync", lambda e, dc4=dc4, wvslab=wvslab: e.dma_start(
                        out=wvslab,
                        in_=g_wv.rearrange("(hc p) d -> p hc d", p=128)[
                            :, :, dc4 * 512:(dc4 + 1) * 512]),
                       waits=wv_extra + [(c_t, vev_mm.get(kv - 2, 0))],
                       inc=d_wv, inc_n=16)
                    wv_ready = d_wv.n
                    for st in range(16):
                        kk = kv * 16 + st
                        ps = psb[4 + kk % 2]
                        mw = [(d_wv, wv_ready),
                              (c_s, vev_ev.get(kk - 2, 0))]

                        def mm(e, xlt=xlt, st=st, ps=ps, wvslab=wvslab):
                            r = None
                            for h2 in range(HC):
                                r = e.matmul(
                                    ps[:],
                                    xlt[:, h2, st * 128:(st + 1) * 128],
                                    wvslab[:, h2, :],
                                    start=(h2 == 0), stop=(h2 == HC - 1))
                            return r
                        op("tensor", mm, waits=mw, inc=c_t)
                        t_mm = c_t.n
                        if st == 15:
                            vev_mm[kv] = t_mm
                        sg = sgb[kk % 2]
                        wvs = w_v[kk % 2]
                        op("scalar", lambda e, ps=ps, sg=sg: e.copy(
                            sg[:], ps[:]),
                           waits=[(c_t, t_mm), (wvs, wvs.n)], inc=c_s)
                        vev_ev[kk] = c_s.n
                        s_ev = c_s.n
                        row = sh * 2048 + st * 128
                        op("scalar", lambda e, r=row, dc4=dc4, sg=sg:
                           e.dma_start(
                               out=g_v[r:r + 128, dc4 * 512:(dc4 + 1) * 512],
                               in_=sg[:]),
                           waits=[(c_s, s_ev)], inc=wvs, inc_n=16)
            p5_end_t = c_t.n
            qk_done = [(w_qk[0], w_qk[0].n), (w_qk[1], w_qk[1].n)]
            v_done = [(w_v[0], w_v[0].n), (w_v[1], w_v[1].n)]

        d_hd = spair("dhd")
        w_cx = spair("wcx")
        p6_end_t = p5_end_t
        if "p6" in phases:
            AH = [arena[:, i * 4096:(i + 1) * 4096] for i in range(6)]
            pT0 = 24576

            def pTs(kt):
                return arena[:, pT0 + kt * 512:pT0 + (kt + 1) * 512]
            ctxb = [arena[:, 40960 + i * 4096:40960 + (i + 1) * 4096]
                    for i in range(2)]
            head_end = {}
            cx_st_war = {}
            zc_last_ev = {0: 0, 1: 0}
            for h in range(NH):
                p = h % 2
                qh, kh, vh = AH[p], AH[2 + p], AH[4 + p]
                vhv = vh.rearrange("p (kc d) -> p kc d", kc=32)
                ld = d_hd[p]
                ws = [(c_t, head_end.get(h - 2, p5_end_t if h < 2 else 0))]
                if h == 0:
                    ws += qk_done + v_done + [(c_g, masks_done_g)]
                op("sync", lambda e, h=h, qh=qh: e.dma_start(
                    out=qh, in_=g_qT[h * 128:(h + 1) * 128, :]),
                   waits=ws, inc=ld, inc_n=16)
                op("sync", lambda e, h=h, kh=kh: e.dma_start(
                    out=kh, in_=g_kT[h * 128:(h + 1) * 128, :]),
                   inc=ld, inc_n=16)
                op("sync", lambda e, h=h, vhv=vhv: e.dma_start(
                    out=vhv,
                    in_=g_v.rearrange("(kc p) d -> p kc d", p=128)[
                        :, :, h * 128:(h + 1) * 128]),
                   inc=ld, inc_n=16)
                loads_rdy = ld.n
                if h == 0 and "g_dbg_q" in DEBUG_TAPS:
                    op("scalar", lambda e, qh=qh: e.dma_start(
                        out=g_dbg_q, in_=qh), waits=[(ld, loads_rdy)],
                       inc=w_wb, inc_n=16)
                    op("scalar", lambda e, kh=kh: e.dma_start(
                        out=g_dbg_k, in_=kh), waits=[(ld, loads_rdy)],
                       inc=w_wb, inc_n=16)
                ctx_end_prev = 0
                for qc in range(P6_NQC):
                    KT = 4 * (qc + 1)
                    if qc > 0:
                        wait("scalar", (c_t, ctx_end_prev))
                    psZ = psb[4 + (qc % 2)]
                    psC = psb[6 + (qc % 2)]
                    zc_war = zc_last_ev[qc % 2]
                    exp_done = {}
                    p_rdy = {}

                    def emit_zc(kt, first, last, psZ=psZ, psC=psC, vhv=vhv,
                                p_rdy=p_rdy, qc=qc):
                        off = max(0, kt - 4 * qc) * 128
                        op("tensor",
                           lambda e, kt=kt, first=first, last=last, off=off:
                           e.matmul(psZ[:, off:512], ones_bf[:],
                                    pTs(kt)[:, off:512],
                                    start=first, stop=last,
                                    skip_group_check=True),
                           waits=[p_rdy[kt]] +
                                 ([(c_v, zc_war)] if first else []))
                        op("tensor",
                           lambda e, kt=kt, first=first, last=last, off=off:
                           e.matmul(psC[:, off:512], vhv[:, kt, :],
                                    pTs(kt)[:, off:512],
                                    start=first, stop=last,
                                    skip_group_check=True),
                           inc=(c_t if last else None))

                    for kt in range(KT):
                        bank = psb[kt % 2]
                        mw = [(c_s, exp_done.get(kt - 2, 0))]
                        if P6_SERIAL:
                            mw = [(c_s, c_s.n), (c_v, c_v.n)]
                        if kt == 0 and qc == 0:
                            mw.append((ld, loads_rdy))
                        soff = max(0, kt - 4 * qc) * 128
                        op("tensor",
                           lambda e, bank=bank, kt=kt, qc=qc, kh=kh, qh=qh,
                           soff=soff:
                           e.matmul(bank[:, soff:512],
                                    kh[:, kt * 128:(kt + 1) * 128],
                                    qh[:, qc * 512 + soff:(qc + 1) * 512],
                                    start=True, stop=True,
                                    skip_group_check=True),
                           waits=mw, inc=c_t)
                        t_sc = c_t.n
                        op("scalar",
                           lambda e, bank=bank, kt=kt, soff=soff:
                           e.activation(
                               pTs(kt)[:, soff:512], bank[:, soff:512],
                               AF.Exp,
                               bias=f32(EXP_BIAS), scale=f32(EXP_SCALE)),
                           waits=[(c_t, t_sc)], inc=c_s)
                        exp_done[kt] = c_s.n
                        diag = kt - 4 * qc
                        if diag >= 0:
                            op("vector", lambda e, kt=kt, diag=diag:
                               e.tensor_tensor(
                                   pTs(kt), pTs(kt),
                                   masks[:, diag * 512:(diag + 1) * 512],
                                   ALU.mult),
                               waits=[(c_s, exp_done[kt])] +
                                     ([(c_g, masks_done_g)]
                                      if (h == 0 and qc == 0) else []),
                               inc=c_v)
                            p_rdy[kt] = (c_v, c_v.n)
                        else:
                            p_rdy[kt] = (c_s, exp_done[kt])
                        if kt >= 2:
                            emit_zc(kt - 2, (kt - 2) == 0, False)
                    emit_zc(KT - 2, KT == 2, False)
                    emit_zc(KT - 1, False, True)
                    ctx_end_prev = c_t.n
                    vw = [(c_t, ctx_end_prev)]
                    if qc == 0:
                        vw.append((w_cx[p], cx_st_war.get(("st", h - 2), 0)))
                    op("vector", lambda e, psZ=psZ: e.reciprocal(
                        rtmp[:, 0:512], psZ[:]), waits=vw)
                    op("vector", lambda e, psC=psC, cb=ctxb[p], qc=qc:
                       e.tensor_tensor(cb[:, qc * 512:(qc + 1) * 512],
                                       psC[:], rtmp[:, 0:512], ALU.mult),
                       inc=c_v)
                    cx_st_war[("ev", h, qc)] = c_v.n
                    zc_last_ev[qc % 2] = c_v.n
                    if h == 0 and qc == 0 and "g_dbg_p" in DEBUG_TAPS:
                        dv = c_v.n
                        op("scalar", lambda e: e.dma_start(
                            out=g_dbg_p, in_=arena[:, pT0:pT0 + 2048]),
                           waits=[(c_v, dv)], inc=w_wb, inc_n=16)
                        op("scalar", lambda e: e.dma_start(
                            out=g_dbg_z, in_=rtmp[:, 0:512]),
                           waits=[(c_v, dv)], inc=w_wb, inc_n=16)
                head_end[h] = c_t.n
                v_cx = c_v.n
                op("scalar", lambda e, h=h, cb=ctxb[p]: e.dma_start(
                    out=g_ctxT[h * 128:(h + 1) * 128, :], in_=cb),
                   waits=[(c_v, v_cx)], inc=w_cx[p], inc_n=16)
                cx_st_war[("st", h)] = w_cx[p].n
            p6_end_t = c_t.n
            cx_done = [(w_cx[0], w_cx[0].n), (w_cx[1], w_cx[1].n)]

        d_cx = spair("dcx")
        p7_end_t = p6_end_t
        ln3_g = {}
        if "p7" in phases:
            wo_sb = arena[:, 0:HC * 2048].rearrange("p (dc o) -> p dc o",
                                                    dc=HC)
            s_now = c_s.n
            for q4 in range(4):
                op("sync", lambda e, q4=q4: e.dma_start(
                    out=wo_sb[:, q4 * 4:(q4 + 1) * 4, :],
                    in_=g_wo.rearrange("(hc p) d -> p hc d", p=128)[
                        :, q4 * 4:(q4 + 1) * 4, :]),
                   waits=([(c_t, p6_end_t), (c_s, s_now)]
                          if q4 == 0 else []),
                   inc=d_wv, inc_n=16)
            wo_ready = d_wv.n
            st_ccol, st_evx, st_evk = {}, {}, {}
            p3_y = {w_y[0]: w_y[0].n, w_y[1]: w_y[1].n}
            for st in range(ST):
                csem = d_cx[st % 2]
                cc = qkbuf[st % 2]
                ccv = cc.rearrange("p (nh s) -> p nh s", nh=NH)
                op("sync", lambda e, st=st, ccv=ccv: e.dma_start(
                    out=ccv,
                    in_=g_ctxT.rearrange("(nh p) s -> p nh s", p=128)[
                        :, :, st * 128:(st + 1) * 128]),
                   waits=(cx_done if st == 0 else []) +
                         [(c_t, st_ccol.get(st - 2, 0))],
                   inc=csem, inc_n=16)
                cc_ready = csem.n
                ld = d_yr[st % 2]
                xb = xtile[st % 2]
                op("sync", lambda e, st=st, xb=xb: e.dma_start(
                    out=xb[:], in_=g_y1[st * 128:(st + 1) * 128, :]),
                   waits=[(c_v, st_evx.get(st - 2, 0))],
                   inc=ld, inc_n=16)
                x_ready = ld.n
                yb = ybuf[st % 2]
                wsy = w_y[st % 2]
                for oc4 in range(4):
                    kk = st * 4 + oc4
                    ps = psb[4 + kk % 2]

                    def mm(e, ccv=ccv, ps=ps, oc4=oc4):
                        r = None
                        for dc in range(HC):
                            r = e.matmul(ps[:], ccv[:, dc, :],
                                         wo_sb[:, dc,
                                               oc4 * 512:(oc4 + 1) * 512],
                                         start=(dc == 0), stop=(dc == HC - 1))
                        return r
                    op("tensor", mm,
                       waits=[(d_wv, wo_ready), (csem, cc_ready),
                              (c_v, st_evk.get(kk - 2, 0))],
                       inc=c_t)
                    t_mm = c_t.n
                    st_ccol[st] = t_mm
                    vw = [(c_t, t_mm), (ld, x_ready)]
                    if oc4 == 0:
                        vw.append((wsy, wsy.n if st >= 2 else p3_y[wsy]))
                        if st >= 2:
                            vw.append(ln3_g[st - 2])
                    op("vector", lambda e, ps=ps, xb=xb, yb=yb, oc4=oc4:
                       e.tensor_tensor(yb[:, oc4 * 512:(oc4 + 1) * 512],
                                       ps[:],
                                       xb[:, oc4 * 512:(oc4 + 1) * 512],
                                       ALU.add),
                       waits=vw, inc=c_v)
                    st_evk[kk] = c_v.n
                st_evx[st] = c_v.n
                v_ev = c_v.n
                op("scalar", lambda e, st=st, yb=yb: e.dma_start(
                    out=g_y2[st * 128:(st + 1) * 128, :], in_=yb[:]),
                   waits=[(c_v, v_ev)], inc=wsy, inc_n=16)
                ln3_g[st] = emit_ln_tile(yb[:], [(c_v, v_ev)],
                                         eng="vector")
            y2_done = [(w_y[0], w_y[0].n), (w_y[1], w_y[1].n)]
            p7_end_t = c_t.n

        if "p8" in phases:
            p8a_end = emit_gate_up("f2", [], [(c_t, p7_end_t)])
            emit_down("f2", g_y2, y_out, w_out, y2_done, p8a_end,
                      ({0: ln3_g[ST - 2], 1: ln3_g[ST - 1]}
                       if ln3_g else {}))
            wait("sync", (w_out[0], w_out[0].n), (w_out[1], w_out[1].n))

        drains = [w_xln, w_h, w_y, w_qk, w_v, w_cx, w_out, d_x]
        for sp in drains:
            wait("sync", (sp[0], sp[0].n), (sp[1], sp[1].n))
        for s1 in (w_wm, w_wb, w_cst, d_lut):
            wait("sync", (s1, s1.n))

        blk = ctx.enter_context(nc.Block())
        for eng in ENGS:
            def runner(e, eng=eng):
                for fn in streams[eng]:
                    fn(e)
            getattr(blk, eng)(runner)

    return nc


def kernel(**inputs):
    x = np.ascontiguousarray(np.asarray(inputs["L0_attn_hidden_states"],
                                        dtype=np.float32))
    wqT = np.ascontiguousarray(np.asarray(inputs["L0_attn_Wq"],
                                          dtype=np.float32).T)
    wkT = np.ascontiguousarray(np.asarray(inputs["L0_attn_Wk"],
                                          dtype=np.float32).T)
    wvT = np.ascontiguousarray(np.asarray(inputs["L0_attn_Wv"],
                                          dtype=np.float32).T)
    woT = np.ascontiguousarray(np.asarray(inputs["L0_attn_Wo"],
                                          dtype=np.float32).T)
    lut = np.ascontiguousarray(np.asarray(inputs["L0_ffn_lut"],
                                          dtype=np.float32).reshape(1, 16))
    gidxT = np.asarray(inputs["L0_ffn_gate_idx"], dtype=np.int32).T
    uidxT = np.asarray(inputs["L0_ffn_up_idx"], dtype=np.int32).T
    didxT = np.asarray(inputs["L0_ffn_down_idx"], dtype=np.int32).T
    dpad = np.zeros((6144, H), np.int32)
    dpad[0:O] = didxT

    nc = build_program()
    in_maps = [{
        "x": np.ascontiguousarray(x[b]),
        "wqs": np.ascontiguousarray(wqT[b * 256:(b + 1) * 256]),
        "wks": np.ascontiguousarray(wkT[b * 256:(b + 1) * 256]),
        "wvs": np.ascontiguousarray(wvT[b * 256:(b + 1) * 256]),
        "wos": np.ascontiguousarray(woT[b * 256:(b + 1) * 256]),
        "lut": lut,
        "guidxT": np.ascontiguousarray(
            gidxT[b * 512:(b + 1) * 512] if b < 4
            else uidxT[(b - 4) * 512:(b - 3) * 512]),
        "didxTs": np.ascontiguousarray(dpad[b * 768:(b + 1) * 768]),
    } for b in range(8)]
    res = run_bass_kernel_spmd(nc, in_maps, core_ids=list(range(8)))
    return np.stack([res.results[b]["y"] for b in range(8)], axis=0)


# version 35 (final)
# speedup vs baseline: 1.0533x; 1.0149x over previous
import numpy as np
import contextlib
import math
import concourse.bass as bass
import concourse.mybir as mybir
from concourse.bass_utils import run_bass_kernel_spmd

F32 = mybir.dt.float32
BF16 = mybir.dt.bfloat16
I32 = mybir.dt.int32
AF = mybir.ActivationFunctionType
ALU = mybir.AluOpType
f32 = np.float32

S = 4096
H = 2048
O = 5504
OC = O // 128
HC = H // 128
NH = 16
ST = S // 128
TWO_PI = float(2 * np.pi)
C1 = float(np.float32(2 * np.pi))
C2 = float(np.float64(2 * np.pi) - np.float64(np.float32(2 * np.pi)))
LN1E4 = float(np.log(10000.0))
EXP_SCALE = 1.0 / math.sqrt(128.0)
EXP_BIAS = -8.0
NEG_MASK = -30000.0

DEBUG_TAPS = ()
P6_SERIAL = False
P1_ILV = True
LN1_ENG = "vector"
P6_NQC = 8

ENGS = ("sync", "scalar", "vector", "tensor", "gpsimd")
ALL_PHASES = ("p0", "p1", "p2", "p3", "p4", "p5", "p6", "p7", "p8")


def build_program(phases=ALL_PHASES):
    nc = bass.Bass("TRN2", target_bir_lowering=False)

    x_in = nc.dram_tensor("x", [S, H], F32, kind="ExternalInput").ap()
    wqs_in = nc.dram_tensor("wqs", [256, H], F32, kind="ExternalInput").ap()
    wks_in = nc.dram_tensor("wks", [256, H], F32, kind="ExternalInput").ap()
    wvs_in = nc.dram_tensor("wvs", [256, H], F32, kind="ExternalInput").ap()
    wos_in = nc.dram_tensor("wos", [256, H], F32, kind="ExternalInput").ap()
    lut_in = nc.dram_tensor("lut", [1, 16], F32, kind="ExternalInput").ap()
    guidxT_in = nc.dram_tensor("guidxT", [512, O], I32,
                               kind="ExternalInput").ap()
    didxT_in = nc.dram_tensor("didxTs", [768, H], I32,
                              kind="ExternalInput").ap()
    y_out = nc.dram_tensor("y", [S, H], F32, kind="ExternalOutput").ap()

    def scr(name, shape, dtype):
        kind = "ExternalOutput" if name in DEBUG_TAPS else "Internal"
        return nc.dram_tensor(name, shape, dtype, kind=kind).ap()

    PIECES = [(0, 256), (256, 256), (512, 512), (1024, 1024),
              (2048, 1024), (3072, 1024), (4096, 1024), (5120, 384)]
    NGU = len(PIECES)
    g_sh = [scr(f"g_sh{b}", [512, w], BF16) for b, (c0, w) in
            enumerate(PIECES)]
    g_gup = [nc.dram_tensor(f"g_gup{b}", [4096, w], BF16,
                            kind="Internal", addr_space="Shared").ap()
             for b, (c0, w) in enumerate(PIECES)]
    g_shd = scr("g_shd", [768, H], BF16)
    g_wdp = nc.dram_tensor("g_wdp", [6144, H], BF16,
                           kind="Internal", addr_space="Shared").ap()
    g_bar_i = scr("g_bar_i", [1, 64], I32)
    g_bar_o = scr("g_bar_o", [1, 64], I32)
    g_h = scr("g_h", [OC * 128, S], BF16)
    g_y1 = scr("g_y1", [S, H], F32)
    g_y2 = scr("g_y2", [S, H], F32)
    g_xln = scr("g_xln", [S, H], BF16)
    g_wq_s = scr("g_wq_s", [256, H], BF16)
    g_wk_s = scr("g_wk_s", [256, H], BF16)
    g_wv_s = scr("g_wv_s", [256, H], BF16)
    g_wo_s = scr("g_wo_s", [256, H], BF16)
    g_wq = nc.dram_tensor("g_wq", [H, H], BF16, kind="Internal",
                          addr_space="Shared").ap()
    g_wk = nc.dram_tensor("g_wk", [H, H], BF16, kind="Internal",
                          addr_space="Shared").ap()
    g_wv = nc.dram_tensor("g_wv", [H, H], BF16, kind="Internal",
                          addr_space="Shared").ap()
    g_wo = nc.dram_tensor("g_wo", [H, H], BF16, kind="Internal",
                          addr_space="Shared").ap()
    g_qT = scr("g_qT", [NH * 128, S], BF16)
    g_kT = scr("g_kT", [NH * 128, S], BF16)
    g_v = scr("g_v", [S, H], BF16)
    g_ctxT = scr("g_ctxT", [NH * 128, S], BF16)
    g_dbg_p = scr("g_dbg_p", [128, 2048], BF16)
    g_dbg_z = scr("g_dbg_z", [128, 512], F32)
    g_dbg_q = scr("g_dbg_q", [128, S], BF16)
    g_dbg_k = scr("g_dbg_k", [128, S], BF16)

    ctx = contextlib.ExitStack()
    with ctx:
        def sbt(name, shape, dtype, **kw):
            return ctx.enter_context(nc.sbuf_tensor(name, shape, dtype, **kw))

        arena = sbt("arena", [128, 51200], BF16, is_transpose_dest=True)
        ar_f32 = arena.bitcast(F32)
        ar_i32 = arena.bitcast(I32)
        wcol = [sbt(f"wcol{i}", [128, 2 * HC * 128], BF16) for i in range(2)]
        hbuf = [sbt(f"hbuf{i}", [128, 2048], BF16) for i in range(2)]
        idxb = [sbt(f"idxb{i}", [128, 1024], I32) for i in range(2)]
        mtmp = sbt("mtmp", [128, 1024], BF16)
        maccs = [sbt(f"macc{i}", [128, 1024], BF16) for i in range(2)]
        xtile = [sbt(f"xtile{i}", [128, 2048], F32) for i in range(2)]
        ybuf = [sbt(f"ybuf{i}", [128, 2048], F32) for i in range(2)]
        cosT = sbt("cosT", [128, S], BF16)
        ssinT = sbt("ssinT", [128, S], BF16)
        rtmp = sbt("rtmp", [128, 1024], F32)
        masks = sbt("masks", [128, 4 * 512], BF16)
        ones_bf = sbt("ones_bf", [128, 128], BF16)
        ones_f = sbt("ones_f", [1, 128], F32)
        lutb = sbt("lutb", [128, 16], F32)
        sgb = [sbt(f"sgb{i}", [128, 512], BF16) for i in range(2)]
        stats = sbt("stats", [128, 16], F32)
        qkbuf = [sbt(f"qkbuf{i}", [128, 2048], BF16) for i in range(2)]
        junk = arena[:, 49152:51200]

        psb = [ctx.enter_context(nc.psum_tensor(f"pb{i}", [128, 512], F32))
               for i in range(8)]

        class Sem:
            __slots__ = ("h", "n")
            _k = [0]

            def __init__(self, name):
                Sem._k[0] += 1
                self.h = ctx.enter_context(
                    nc.semaphore(f"{name}{Sem._k[0]}"))
                self.n = 0

        streams = {e: [] for e in ENGS}

        def op(eng, fn, waits=(), inc=None, inc_n=1):
            if inc is not None:
                inc.n += inc_n
            ws = tuple((s.h, v) for s, v in waits if v > 0)
            tgt = (inc.h, inc_n) if inc is not None else None

            def closure(e, fn=fn, ws=ws, tgt=tgt):
                for h, v in ws:
                    e.wait_ge(h, v)
                inst = fn(e)
                if tgt is not None:
                    inst.then_inc(*tgt)
            streams[eng].append(closure)

        def wait(eng, *waits):
            ws = tuple((s.h, v) for s, v in waits if v > 0)

            def closure(e, ws=ws):
                for h, v in ws:
                    e.wait_ge(h, v)
            streams[eng].append(closure)

        def spair(name):
            return (Sem(name), Sem(name))

        c_v, c_s, c_t, c_g = Sem("cv"), Sem("cs"), Sem("ct"), Sem("cg")

        d_lut = Sem("dlut")
        lutb_ready = 0
        tables_done_s = 0
        masks_done_g = 0
        def emit_mask_consts():
            op("gpsimd", lambda e: e.memset(ones_bf[:], 1.0), inc=c_g)
            op("gpsimd", lambda e: e.memset(masks[:], 1.0), inc=c_g)
            for j in range(4):
                op("gpsimd",
                   lambda e, j=j: e.affine_select(
                       masks[:, j * 512:(j + 1) * 512],
                       masks[:, j * 512:(j + 1) * 512],
                       [[1, 512]], ALU.is_ge, 0.0,
                       base=-128 * j, channel_multiplier=-1),
                   inc=c_g)
            return c_g.n

        if "p0" in phases:
            op("gpsimd", lambda e: e.memset(ones_f[:], 1.0), inc=c_g)
            g_consts = c_g.n
            op("sync", lambda e: e.dma_start(out=lutb[0:1, :], in_=lut_in),
               inc=d_lut, inc_n=16)
            op("tensor",
               lambda e: e.matmul(psb[7][:, 0:16], ones_f[:], lutb[0:1, :],
                                  start=True, stop=True),
               waits=[(d_lut, 16), (c_g, g_consts)], inc=c_t)
            op("scalar", lambda e: e.copy(lutb[:], psb[7][:, 0:16]),
               waits=[(c_t, c_t.n)], inc=c_s)
            lutb_ready = c_s.n

            tmpf = ar_f32[:, 0:4096]
            tmpf2 = ar_f32[:, 4096:8192]
            ki = ar_i32[:, 8192:12288]
            kf = ar_f32[:, 12288:16384]
            sinf = ar_f32[:, 16384:20480]
            op("gpsimd",
               lambda e: e.iota(stats[:, 7:8], [[0, 1]], base=0,
                                channel_multiplier=1,
                                allow_small_or_imprecise_dtypes=True),
               inc=c_g)
            op("gpsimd",
               lambda e: e.tensor_scalar(stats[:, 6:7], stats[:, 7:8], 64.0,
                                         64.0, ALU.is_ge, ALU.mult))
            op("gpsimd",
               lambda e: e.tensor_tensor(stats[:, 7:8], stats[:, 7:8],
                                         stats[:, 6:7], ALU.subtract),
               inc=c_g)
            g_rr = c_g.n
            op("scalar",
               lambda e: e.activation(stats[:, 6:7], stats[:, 7:8], AF.Exp,
                                      scale=f32(-LN1E4 / 64.0)),
               waits=[(c_g, g_rr)], inc=c_s)
            s_invf = c_s.n
            op("gpsimd",
               lambda e: e.iota(tmpf, [[1, 4096]], base=0,
                                channel_multiplier=0,
                                allow_small_or_imprecise_dtypes=True),
               inc=c_g)
            g_t = c_g.n

            def sin_table(dst, shift):
                w0 = [(c_g, g_t), (c_s, max(s_invf, c_s.n))]
                if shift == 0.0:
                    op("vector",
                       lambda e: e.tensor_single_scalar(
                           tmpf2, tmpf, stats[:, 6:7], ALU.mult),
                       waits=w0)
                else:
                    op("vector",
                       lambda e, sh=shift: e.tensor_scalar(
                           tmpf2, tmpf, stats[:, 6:7], sh, ALU.mult, ALU.add),
                       waits=w0)
                op("vector", lambda e: e.tensor_single_scalar(
                    sinf, tmpf2, 1.0 / TWO_PI, ALU.mult))
                op("vector", lambda e: e.tensor_copy(ki, sinf))
                op("vector", lambda e: e.tensor_copy(kf, ki))
                op("vector", lambda e: e.scalar_tensor_tensor(
                    sinf, kf, -C1, tmpf2, ALU.mult, ALU.add))
                op("vector", lambda e: e.scalar_tensor_tensor(
                    sinf, kf, -C2, sinf, ALU.mult, ALU.add), inc=c_v)
                vr = c_v.n
                op("scalar", lambda e, d=dst: e.activation(d, sinf, AF.Sin),
                   waits=[(c_v, vr)], inc=c_s)
                return c_s.n

            sin_table(cosT[:], math.pi / 2)
            s_tab = sin_table(ssinT[:], 0.0)
            tables_done_s = s_tab
            op("vector",
               lambda e: e.tensor_single_scalar(ssinT[0:64, :],
                                                ssinT[0:64, :],
                                                -1.0, ALU.mult),
               waits=[(c_s, s_tab)], inc=c_v)
            tables_done_v = c_v.n
        else:
            tables_done_v = 0

        CH = 1024
        mat_jobs = []
        piece_last_job = []
        for b, (c0, w) in enumerate(PIECES):
            for t in range(4):
                mat_jobs.append(
                    (guidxT_in[t * 128:(t + 1) * 128, c0:c0 + w],
                     g_sh[b][t * 128:(t + 1) * 128, :], w))
            piece_last_job.append(len(mat_jobs) - 1)
        for t in range(6):
            for c0 in (0, 1024):
                mat_jobs.append(
                    (didxT_in[t * 128:(t + 1) * 128, c0:c0 + CH],
                     g_shd[t * 128:(t + 1) * 128, c0:c0 + CH], CH))
        piece_last_job.append(len(mat_jobs) - 1)

        d_idx = spair("didx")
        w_wm = Sem("wwm")
        w_cst = Sem("wcst")
        cc_sem = Sem("ccw")

        if "p0" in phases:
            for w_in, w_sh in ((wqs_in, g_wq_s), (wks_in, g_wk_s),
                               (wvs_in, g_wv_s), (wos_in, g_wo_s)):
                op("gpsimd", lambda e, a=w_in, b=w_sh: e.dma_start(
                    out=b, in_=a), inc=w_cst, inc_n=16)
            for piece in range(NGU + 1):
                src = g_sh[piece].opt() if piece < NGU else g_shd.opt()
                dst = g_gup[piece].opt() if piece < NGU else g_wdp.opt()
                op("gpsimd", lambda e, src=src, dst=dst: e.collective_compute(
                    "AllGather", ALU.bypass,
                    replica_groups=[[0, 1, 2, 3, 4, 5, 6, 7]],
                    ins=[src], outs=[dst]),
                   waits=[(w_wm, 16 * (piece_last_job[piece] + 1))],
                   inc=cc_sem)
            for w_sh, w_full in ((g_wq_s, g_wq), (g_wk_s, g_wk),
                                 (g_wv_s, g_wv), (g_wo_s, g_wo)):
                op("gpsimd", lambda e, a=w_sh, b=w_full: e.collective_compute(
                    "AllGather", ALU.bypass,
                    replica_groups=[[0, 1, 2, 3, 4, 5, 6, 7]],
                    ins=[a.opt()], outs=[b.opt()]),
                   waits=[(w_cst, 64)], inc=cc_sem)
            op("gpsimd", lambda e: e.collective_compute(
                "AllReduce", ALU.add,
                replica_groups=[[0, 1, 2, 3, 4, 5, 6, 7]],
                ins=[g_bar_i.opt()], outs=[g_bar_o.opt()]),
               inc=cc_sem)
        cc_all = cc_sem.n

        mat_state = {"k": 0, "j": 0, "read_done": {}}
        NMJ = len(mat_jobs) if "p0" in phases else 0

        def emit_idx_load(k):
            in_ap, _, w = mat_jobs[k]
            rd = mat_state["read_done"].get(k - 2, 0)
            op("scalar", lambda e, a=in_ap, b=idxb[k % 2], w=w:
               e.dma_start(out=b[:, 0:w], in_=a),
               waits=[(c_v, rd)], inc=d_idx[k % 2], inc_n=16)

        def pump(n):
            for _ in range(n):
                k = mat_state["k"]
                if k >= NMJ:
                    return
                j = mat_state["j"]
                in_ap, out_ap, w = mat_jobs[k]
                buf = idxb[k % 2]
                mc = maccs[k % 2]
                if j == 0:
                    op("vector", lambda e, buf=buf, mc=mc, w=w:
                       e.tensor_scalar(mc[:, 0:w], buf[:, 0:w], 0.0,
                                       lutb[:, 0:1], ALU.is_equal, ALU.mult),
                       waits=[(d_idx[k % 2], 16 * (k // 2 + 1)),
                              (c_s, lutb_ready),
                              (w_wm, 16 * max(0, k - 1))])
                elif j % 2 == 1:
                    jj = (j + 1) // 2
                    op("vector", lambda e, buf=buf, jj=jj, w=w:
                       e.tensor_scalar(mtmp[:, 0:w], buf[:, 0:w], float(jj),
                                       lutb[:, jj:jj + 1],
                                       ALU.is_equal, ALU.mult),
                       inc=(c_v if j == 29 else None))
                    if j == 29:
                        mat_state["read_done"][k] = c_v.n
                else:
                    op("vector", lambda e, mc=mc, w=w: e.tensor_tensor(
                        mc[:, 0:w], mc[:, 0:w], mtmp[:, 0:w], ALU.add),
                       inc=(c_v if j == 30 else None))
                if j == 30:
                    op("scalar", lambda e, o=out_ap, mc=mc, w=w: e.dma_start(
                        out=o, in_=mc[:, 0:w]),
                       waits=[(c_v, c_v.n)], inc=w_wm, inc_n=16)
                    if k + 2 < NMJ:
                        emit_idx_load(k + 2)
                    mat_state["k"] = k + 1
                    mat_state["j"] = 0
                else:
                    mat_state["j"] = j + 1

        w_xln = spair("wxln")
        ln_state = {"i": 0, "tile_store": {}}

        def ln_a(i, src_ap, src_waits):
            cb = (i % 2) * 8
            op("scalar", lambda e, a=src_ap, cb=cb: e.activation(
                junk[:], a, AF.Copy, accum_out=stats[:, cb:cb + 1]),
               waits=src_waits, inc=c_s)
            op("scalar", lambda e, a=src_ap, cb=cb: e.activation(
                junk[:], a, AF.Square, accum_out=stats[:, cb + 1:cb + 2]),
               inc=c_s)
            op("scalar", lambda e, cb=cb: e.mul(
                stats[:, cb + 2:cb + 3], stats[:, cb:cb + 1], 1.0 / H))
            op("scalar", lambda e, cb=cb: e.mul(
                stats[:, cb + 3:cb + 4], stats[:, cb + 1:cb + 2], 1.0 / H),
               inc=c_s)
            return c_s.n

        def ln_b(i, src_ap, s_stats, eng="gpsimd"):
            cx = c_g if eng == "gpsimd" else c_v
            cb = (i % 2) * 8
            op(eng, lambda e, cb=cb: e.tensor_tensor(
                stats[:, cb + 7:cb + 8], stats[:, cb + 2:cb + 3],
                stats[:, cb + 2:cb + 3], ALU.mult),
               waits=[(c_s, s_stats)], inc=cx)
            wait(eng, (cx, cx.n))
            op(eng, lambda e, cb=cb: e.tensor_tensor(
                stats[:, cb + 3:cb + 4], stats[:, cb + 3:cb + 4],
                stats[:, cb + 7:cb + 8], ALU.subtract), inc=cx)
            wait(eng, (cx, cx.n))
            op(eng, lambda e, cb=cb: e.tensor_single_scalar(
                stats[:, cb + 5:cb + 6], stats[:, cb + 2:cb + 3],
                -1.0, ALU.mult), inc=cx)
            g_st = cx.n
            op("scalar", lambda e, cb=cb: e.activation(
                stats[:, cb + 4:cb + 5], stats[:, cb + 3:cb + 4], AF.Ln,
                bias=f32(1e-5)),
               waits=[(cx, g_st)])
            op("scalar", lambda e, cb=cb: e.activation(
                stats[:, cb + 4:cb + 5], stats[:, cb + 4:cb + 5], AF.Exp,
                scale=f32(-0.5)),
               inc=c_s)
            s_rstd = c_s.n
            dst = hbuf[i % 2]
            ws = w_xln[i % 2]
            op(eng, lambda e, a=src_ap, d=dst, cb=cb: e.tensor_scalar(
                d[:], a, stats[:, cb + 5:cb + 6], stats[:, cb + 4:cb + 5],
                ALU.add, ALU.mult),
               waits=[(c_s, s_rstd), (ws, ws.n)], inc=cx)
            g_norm = cx.n
            row = (i % ST) * 128
            op("scalar", lambda e, r=row, d=dst: e.dma_start(
                out=g_xln[r:r + 128, :], in_=d[:]),
               waits=[(cx, g_norm)], inc=ws, inc_n=16)
            ln_state["tile_store"][i % ST] = (ws, ws.n)
            return (cx, cx.n)

        def emit_ln_tile(src_ap, src_waits, eng="gpsimd"):
            i = ln_state["i"]
            ln_state["i"] = i + 1
            s_stats = ln_a(i, src_ap, src_waits)
            return ln_b(i, src_ap, s_stats, eng)

        d_x = spair("dx")
        w_wb = Sem("wwb")
        if "p1" in phases:
            if NMJ:
                emit_idx_load(0)
                emit_idx_load(1)
            x_rd = {}
            pend = {}
            for i in range(ST):
                ld = d_x[i % 2]
                buf = xtile[i % 2]
                w = x_rd.get(i - 2)
                op("sync" if i < 16 else "scalar",
                   lambda e, i=i, b=buf: e.dma_start(
                       out=b[:], in_=x_in[i * 128:(i + 1) * 128, :]),
                   waits=[w] if w else [], inc=ld, inc_n=16)
                ln_state["i"] = i + 1
                pend[i] = (buf, ln_a(i, buf[:], [(ld, ld.n)]))
                if i >= 1:
                    b1, s1 = pend.pop(i - 1)
                    x_rd[i - 1] = ln_b(i - 1, b1[:], s1, eng=LN1_ENG)
                pump(31 if mat_state["k"] < 8 else (12 if i < 16 else 16))
            b1, s1 = pend.pop(ST - 1)
            x_rd[ST - 1] = ln_b(ST - 1, b1[:], s1, eng=LN1_ENG)

        if "p0" in phases:
            masks_done_g = emit_mask_consts()
        if "t_gup" in DEBUG_TAPS and "p0" in phases:
            t_gup = [nc.dram_tensor(f"t_gup{b}", [4096, w], BF16,
                                    kind="ExternalOutput").ap()
                     for b, (c0, w) in enumerate(PIECES)]
            t_wdp = nc.dram_tensor("t_wdp", [6144, H], BF16,
                                   kind="ExternalOutput").ap()
            for b in range(NGU):
                op("scalar", lambda e, b=b: e.dma_start(
                    out=t_gup[b], in_=g_gup[b]),
                   waits=[(cc_sem, b + 1)], inc=w_wb, inc_n=16)
            op("scalar", lambda e: e.dma_start(out=t_wdp, in_=g_wdp),
               waits=[(cc_sem, 7)], inc=w_wb, inc_n=16)

        d_xlt = Sem("dxlt")
        d_wc = spair("dwc")
        w_h = spair("wh")

        def emit_gate_up(tag, side_jobs, arena_free):
            xlt = arena[:, 0:HC * 2048].rearrange("p (hc s) -> p hc s", hc=HC)
            ts = dict(ln_state["tile_store"])
            xln_half = [[ts[i] for i in (14, 15) if i in ts],
                        [ts[i] for i in (30, 31) if i in ts]]
            st_wcol, st_evac, st_hst = {}, {}, {}
            for sh in range(2):
                ct_now = c_t.n
                if sh == 0:
                    ws = xln_half[0] + list(arena_free)
                else:
                    ws = xln_half[1] + [(c_t, ct_now)]
                op("sync",
                   lambda e, sh=sh: e.dma_start_transpose(
                       xlt[:, :, :],
                       g_xln[sh * 2048:(sh + 1) * 2048, :]),
                   waits=ws, inc=d_xlt, inc_n=16)
                xlt_done = d_xlt.n
                for oc in range(OC):
                    k = sh * OC + oc
                    wsem = d_wc[k % 2]
                    wb = wcol[k % 2]
                    cb = next(p for p, (c0, pw) in enumerate(PIECES)
                              if c0 <= oc * 128 < c0 + pw)
                    co = oc * 128 - PIECES[cb][0]
                    blk_ws = [(cc_sem, cb + 1)] if sh == 0 else []
                    op("sync", lambda e, cb=cb, co=co, wb=wb: e.dma_start(
                        out=wb[:, 0:HC * 128].rearrange(
                            "p (hc o) -> p hc o", hc=HC),
                        in_=g_gup[cb][0:2048].rearrange(
                            "(hc p) o -> p hc o", p=128)[:, :, co:co + 128]),
                       waits=blk_ws + [(c_t, st_wcol.get(k - 2, 0))],
                       inc=wsem, inc_n=16)
                    op("sync", lambda e, cb=cb, co=co, wb=wb: e.dma_start(
                        out=wb[:, HC * 128:2 * HC * 128].rearrange(
                            "p (hc o) -> p hc o", hc=HC),
                        in_=g_gup[cb][2048:4096].rearrange(
                            "(hc p) o -> p hc o", p=128)[:, :, co:co + 128]),
                       inc=wsem, inc_n=16)
                    w_ready = wsem.n
                    for sc in range(4):
                        slot = k * 4 + sc
                        pg = psb[slot % 4]
                        pu = psb[4 + slot % 4]
                        mw = [(wsem, w_ready),
                              (c_v, st_evac.get(slot - 4, 0))]
                        if slot == 3:
                            mw.append((c_s, lutb_ready))
                        if oc == 0 and sc == 0:
                            mw.append((d_xlt, xlt_done))

                        def mm(e, xlt=xlt, wb=wb, pg=pg, pu=pu, sc=sc):
                            for h2 in range(HC):
                                e.matmul(pg[:], wb[:, h2 * 128:(h2 + 1) * 128],
                                         xlt[:, h2, sc * 512:(sc + 1) * 512],
                                         start=(h2 == 0), stop=(h2 == HC - 1))
                            r = None
                            for h2 in range(HC):
                                r = e.matmul(
                                    pu[:],
                                    wb[:, (HC + h2) * 128:(HC + h2 + 1) * 128],
                                    xlt[:, h2, sc * 512:(sc + 1) * 512],
                                    start=(h2 == 0), stop=(h2 == HC - 1))
                            return r
                        op("tensor", mm, waits=mw, inc=c_t)
                        t_mm = c_t.n
                        st_wcol[k] = t_mm
                        sg = sgb[slot % 2]
                        op("scalar", lambda e, pg=pg, sg=sg: e.activation(
                            sg[:], pg[:], AF.Silu),
                           waits=[(c_t, t_mm),
                                  (c_v, st_evac.get(slot - 2, 0))],
                           inc=c_s)
                        s_silu = c_s.n
                        hb = hbuf[k % 2]
                        hw = w_h[k % 2]
                        vw = [(c_s, s_silu)]
                        if sc == 0:
                            vw.append((hw, hw.n))
                            if k < 2:
                                vw.append((w_xln[0], w_xln[0].n))
                                vw.append((w_xln[1], w_xln[1].n))
                        op("vector",
                           lambda e, hb=hb, sg=sg, pu=pu, sc=sc:
                           e.tensor_tensor(hb[:, sc * 512:(sc + 1) * 512],
                                           sg[:], pu[:], ALU.mult),
                           waits=vw, inc=c_v)
                        st_evac[slot] = c_v.n
                        pump(6)
                    v_h = c_v.n
                    op("scalar", lambda e, oc=oc, sh=sh, hb=hbuf[k % 2]:
                       e.dma_start(
                           out=g_h[oc * 128:(oc + 1) * 128,
                                   sh * 2048:(sh + 1) * 2048],
                           in_=hb[:]),
                       waits=[(c_v, v_h)], inc=hw, inc_n=16)
                    st_hst[k] = hw.n
            return c_t.n

        d_wd = Sem("dwd")
        d_hc = spair("dhc")
        d_xq = spair("dxq")
        w_y = spair("wy")
        w_out = spair("wout")

        down_store = {}

        def emit_down(tag, res_dram, out_dram, wsems, pre_waits, arena_free_t,
                      ybuf_war, tile_cb=None):
            wd_sb = arena[:, 0:OC * 512].rearrange("p (oc d) -> p oc d", oc=OC)
            hb0 = OC * 512
            hcols = [arena[:, hb0 + i * O: hb0 + (i + 1) * O]
                     .rearrange("p (oc s) -> p oc s", oc=OC) for i in range(2)]
            st_hcol, st_evac, st_xq = {}, {}, {}
            wd_prev_used = arena_free_t
            h_now = [(w_h[0], w_h[0].n), (w_h[1], w_h[1].n)]
            OCS = [(0, 11), (11, 11), (22, 11), (33, 10)]
            for dq in range(4):
                wd_subs = []
                for sj, (o0, on) in enumerate(OCS):
                    op("sync", lambda e, dq=dq, o0=o0, on=on: e.dma_start(
                        out=wd_sb[:, o0:o0 + on, :],
                        in_=g_wdp.rearrange("(oc p) d -> p oc d", p=128)[
                            :, o0:o0 + on, dq * 512:(dq + 1) * 512]),
                       waits=((list(pre_waits) + h_now if dq == 0 else []) +
                              [(c_t, wd_prev_used)]) if sj == 0 else [],
                       inc=d_wd, inc_n=16)
                    wd_subs.append(d_wd.n)
                wd_ready = d_wd.n
                for st in range(ST):
                    k = dq * ST + st
                    hsem = d_hc[k % 2]
                    op("sync", lambda e, st=st, hc=hcols[k % 2]: e.dma_start(
                        out=hc,
                        in_=g_h.rearrange("(oc p) s -> p oc s", p=128)[
                            :, :, st * 128:(st + 1) * 128]),
                       waits=[(c_t, st_hcol.get(
                           k - 2, arena_free_t if k < 2 else 0))],
                       inc=hsem, inc_n=16)
                    h_ready = hsem.n
                    xq = d_xq[k % 2]
                    xb = xtile[k % 2]
                    op("sync", lambda e, st=st, dq=dq, xb=xb: e.dma_start(
                        out=xb[:, 0:512],
                        in_=res_dram[st * 128:(st + 1) * 128,
                                     dq * 512:(dq + 1) * 512]),
                       waits=[(c_v, st_xq.get(k - 2, 0))], inc=xq, inc_n=16)
                    x_ready = xq.n
                    ps = psb[4 + (k % 2)]

                    subw = ([(o0, wd_subs[sj]) for sj, (o0, on) in
                             enumerate(OCS)] if st == 0 else [])

                    def mm(e, hc=hcols[k % 2], ps=ps, subw=subw):
                        r = None
                        bnd = dict(subw)
                        for oc in range(OC):
                            if oc in bnd:
                                e.wait_ge(d_wd.h, bnd[oc])
                            r = e.matmul(ps[:], hc[:, oc, :], wd_sb[:, oc, :],
                                         start=(oc == 0), stop=(oc == OC - 1))
                        return r
                    op("tensor", mm,
                       waits=([(d_wd, wd_ready)] if st > 0 else []) +
                             [(hsem, h_ready),
                              (c_v, st_evac.get(k - 2, 0))],
                       inc=c_t)
                    t_mm = c_t.n
                    st_hcol[k] = t_mm
                    if st == ST - 1:
                        wd_prev_used = t_mm
                    yb = ybuf[st % 2]
                    ysem = wsems[k % 2]
                    vw = [(c_t, t_mm), (xq, x_ready), (ysem, ysem.n)]
                    if k < 2 and (k % 2) in ybuf_war:
                        vw.append(ybuf_war[k % 2])
                    op("vector", lambda e, ps=ps, xb=xb, yb=yb, dq=dq:
                       e.scalar_tensor_tensor(
                           yb[:, dq * 512:(dq + 1) * 512], ps[:], 0.5,
                           xb[:, 0:512], ALU.mult, ALU.add),
                       waits=vw, inc=c_v)
                    st_evac[k] = c_v.n
                    st_xq[k] = c_v.n
                    v_ev = c_v.n
                    op("scalar", lambda e, st=st, dq=dq, yb=yb: e.dma_start(
                        out=out_dram[st * 128:(st + 1) * 128,
                                     dq * 512:(dq + 1) * 512],
                        in_=yb[:, dq * 512:(dq + 1) * 512]),
                       waits=[(c_v, v_ev)], inc=ysem, inc_n=16)
                    down_store[(tag, dq, st)] = (ysem, ysem.n)
                    if tile_cb is not None and dq == 3:
                        tile_cb(st)
            return c_t.n

        d_yr = spair("dyr")
        ln2_state = {"y1_rd": {}, "pend": {}}

        def ln2_tile(i):
            ld = d_yr[i % 2]
            y4buf = [ar_f32[:, 16512:18560], ar_f32[:, 18560:20608]]
            buf = y4buf[i % 2]
            y1_rd = ln2_state["y1_rd"]
            pend4 = ln2_state["pend"]
            op("gpsimd", lambda e, i=i, b=buf: e.dma_start(
                out=b, in_=g_y1[i * 128:(i + 1) * 128, :]),
               waits=[down_store[("f1", 3, i)]] +
                     ([y1_rd[i - 2]] if i - 2 in y1_rd else []),
               inc=ld, inc_n=16)
            ln_state["i"] += 1
            li = ln_state["i"] - 1
            pend4[i] = (li, buf, ln_a(li, buf, [(ld, ld.n)]))
            if i >= 1:
                li1, b1, s1 = pend4.pop(i - 1)
                y1_rd[i - 1] = ln_b(li1, b1, s1, eng="vector")
            if i == ST - 1:
                li1, b1, s1 = pend4.pop(ST - 1)
                y1_rd[ST - 1] = ln_b(li1, b1, s1, eng="vector")

        p2_end = 0
        if "p2" in phases:
            p2_end = emit_gate_up(
                "f1", [], [(c_s, tables_done_s), (c_v, tables_done_v)])
        while mat_state["k"] < NMJ:
            pump(31)
        p3_end = p2_end
        if "p3" in phases:
            p3_end = emit_down("f1", x_in, g_y1, w_y,
                               [(cc_sem, min(NGU + 1, cc_all))], p2_end, {},
                               tile_cb=(ln2_tile if "p4" in phases
                                        else None))
            y1_done = [(w_y[0], w_y[0].n), (w_y[1], w_y[1].n)]



        d_wv = Sem("dwv")
        w_qk = spair("wqk")
        w_v = spair("wv")
        p5_end_t = p3_end
        if "p5" in phases:
            xlt = arena[:, 0:HC * 2048].rearrange("p (hc s) -> p hc s", hc=HC)
            wvslabs = [arena[:, 32768 + i * 8192:32768 + (i + 1) * 8192]
                       .rearrange("p (hc d) -> p hc d", hc=HC)
                       for i in range(2)]
            st_wcol, st_rope = {}, {}
            vev_mm, vev_ev = {}, {}
            qk_war = {}
            ts5 = dict(ln_state["tile_store"])
            xln_half5 = [[ts5[i] for i in (14, 15) if i in ts5],
                         [ts5[i] for i in (30, 31) if i in ts5]]
            wb0 = wcol[0]
            op("sync", lambda e, wb=wb0: e.dma_start(
                out=wb[:, 0:HC * 128].rearrange("p (hc d) -> p hc d", hc=HC),
                in_=g_wq.rearrange("(hc p) d -> p hc d", p=128)[:, :, 0:128]),
               waits=[(c_t, p2_end), (cc_sem, cc_all)],
               inc=d_wc[0], inc_n=16)
            op("sync", lambda e, wb=wb0: e.dma_start(
                out=wb[:, HC * 128:2 * HC * 128].rearrange(
                    "p (hc d) -> p hc d", hc=HC),
                in_=g_wk.rearrange("(hc p) d -> p hc d", p=128)[:, :, 0:128]),
               inc=d_wc[0], inc_n=16)
            prefetched = {"done": False}
            for sh in range(2):
                ct_now5 = c_t.n
                if sh == 0:
                    ws = xln_half5[0] + [(c_t, p3_end)]
                else:
                    ws = xln_half5[1] + [(c_t, ct_now5)]
                op("sync",
                   lambda e, sh=sh: e.dma_start_transpose(
                       xlt[:, :, :],
                       g_xln[sh * 2048:(sh + 1) * 2048, :]),
                   waits=ws, inc=d_xlt, inc_n=16)
                xlt_done = d_xlt.n
                for dc in range(NH):
                    k = sh * NH + dc
                    wsem = d_wc[k % 2]
                    wb = wcol[k % 2]
                    if k == 0:
                        w_ready = wsem.n
                    else:
                        op("sync", lambda e, dc=dc, wb=wb: e.dma_start(
                            out=wb[:, 0:HC * 128].rearrange(
                                "p (hc d) -> p hc d", hc=HC),
                            in_=g_wq.rearrange("(hc p) d -> p hc d", p=128)[
                                :, :, dc * 128:(dc + 1) * 128]),
                           waits=[(c_t, st_wcol.get(
                               k - 2, p2_end if k < 2 else 0))],
                           inc=wsem, inc_n=16)
                        op("sync", lambda e, dc=dc, wb=wb: e.dma_start(
                            out=wb[:, HC * 128:2 * HC * 128].rearrange(
                                "p (hc d) -> p hc d", hc=HC),
                            in_=g_wk.rearrange("(hc p) d -> p hc d", p=128)[
                                :, :, dc * 128:(dc + 1) * 128]),
                           inc=wsem, inc_n=16)
                        w_ready = wsem.n
                    for qk in range(2):
                        qbuf = qkbuf[qk]
                        wqksem = w_qk[qk]
                        for sc in range(4):
                            slot = (k * 2 + qk) * 4 + sc
                            ps = psb[(slot % 2) + qk * 2]
                            mw = [(wsem, w_ready),
                                  (c_v, st_rope.get((qk, slot - 2), 0))]
                            if dc == 0 and qk == 0 and sc == 0:
                                mw.append((d_xlt, xlt_done))

                            def mm(e, xlt=xlt, wb=wb, ps=ps, sc=sc, qk=qk):
                                r = None
                                for h2 in range(HC):
                                    r = e.matmul(
                                        ps[:],
                                        wb[:, (qk * HC + h2) * 128:
                                           (qk * HC + h2 + 1) * 128],
                                        xlt[:, h2, sc * 512:(sc + 1) * 512],
                                        start=(h2 == 0), stop=(h2 == HC - 1))
                                return r
                            op("tensor", mm, waits=mw, inc=c_t)
                            t_mm = c_t.n
                            st_wcol[k] = t_mm
                            gofs = sh * 2048 + sc * 512
                            t1 = rtmp[:, 0:512]
                            t2 = rtmp[:, 512:1024]
                            vw = [(c_t, t_mm)]
                            if sc == 0:
                                vw.append((wqksem, qk_war.get((qk, k - 2), 0)))
                            if k == 0 and qk == 0 and sc == 0:
                                vw.append((c_v, tables_done_v))
                            op("vector",
                               lambda e, ps=ps, g=gofs: e.tensor_tensor(
                                   t1, ps[:], cosT[:, g:g + 512], ALU.mult),
                               waits=vw)
                            op("vector",
                               lambda e, ps=ps, g=gofs: e.tensor_tensor(
                                   t2[0:64, :], ps[64:128, :],
                                   ssinT[0:64, g:g + 512], ALU.mult))
                            op("vector",
                               lambda e, ps=ps, g=gofs: e.tensor_tensor(
                                   t2[64:128, :], ps[0:64, :],
                                   ssinT[64:128, g:g + 512], ALU.mult))
                            op("vector", lambda e, qbuf=qbuf, sc=sc:
                               e.tensor_tensor(
                                   qbuf[:, sc * 512:(sc + 1) * 512],
                                   t1, t2, ALU.add),
                               inc=c_v)
                            st_rope[(qk, slot)] = c_v.n
                        v_r = c_v.n
                        dstg = g_qT if qk == 0 else g_kT
                        op("scalar",
                           lambda e, d=dstg, dc=dc, sh=sh, qbuf=qbuf:
                           e.dma_start(
                               out=d[dc * 128:(dc + 1) * 128,
                                     sh * 2048:(sh + 1) * 2048],
                               in_=qbuf[:]),
                           waits=[(c_v, v_r)], inc=wqksem, inc_n=16)
                        qk_war[(qk, k)] = wqksem.n
                for dc4 in range(4):
                    kv = sh * 4 + dc4
                    wvslab = wvslabs[kv % 2]
                    wv_extra = ([(w_xln[0], w_xln[0].n),
                                 (w_xln[1], w_xln[1].n),
                                 (c_g, c_g.n), (c_v, c_v.n)]
                                if kv < 2 else [])
                    op("sync", lambda e, dc4=dc4, wvslab=wvslab: e.dma_start(
                        out=wvslab,
                        in_=g_wv.rearrange("(hc p) d -> p hc d", p=128)[
                            :, :, dc4 * 512:(dc4 + 1) * 512]),
                       waits=wv_extra + [(c_t, vev_mm.get(kv - 2, 0))],
                       inc=d_wv, inc_n=16)
                    wv_ready = d_wv.n
                    for st in range(16):
                        kk = kv * 16 + st
                        ps = psb[4 + kk % 2]
                        mw = [(d_wv, wv_ready),
                              (c_s, vev_ev.get(kk - 2, 0))]

                        def mm(e, xlt=xlt, st=st, ps=ps, wvslab=wvslab):
                            r = None
                            for h2 in range(HC):
                                r = e.matmul(
                                    ps[:],
                                    xlt[:, h2, st * 128:(st + 1) * 128],
                                    wvslab[:, h2, :],
                                    start=(h2 == 0), stop=(h2 == HC - 1))
                            return r
                        op("tensor", mm, waits=mw, inc=c_t)
                        t_mm = c_t.n
                        if st == 15:
                            vev_mm[kv] = t_mm
                        sg = sgb[kk % 2]
                        wvs = w_v[kk % 2]
                        op("scalar", lambda e, ps=ps, sg=sg: e.copy(
                            sg[:], ps[:]),
                           waits=[(c_t, t_mm), (wvs, wvs.n)], inc=c_s)
                        vev_ev[kk] = c_s.n
                        s_ev = c_s.n
                        row = sh * 2048 + st * 128
                        op("scalar", lambda e, r=row, dc4=dc4, sg=sg:
                           e.dma_start(
                               out=g_v[r:r + 128, dc4 * 512:(dc4 + 1) * 512],
                               in_=sg[:]),
                           waits=[(c_s, s_ev)], inc=wvs, inc_n=16)
            p5_end_t = c_t.n
            qk_done = [(w_qk[0], w_qk[0].n), (w_qk[1], w_qk[1].n)]
            v_done = [(w_v[0], w_v[0].n), (w_v[1], w_v[1].n)]

        d_hd = spair("dhd")
        w_cx = spair("wcx")
        p6_end_t = p5_end_t
        if "p6" in phases:
            AH = [arena[:, i * 4096:(i + 1) * 4096] for i in range(6)]
            pT0 = 24576

            def pTs(kt):
                return arena[:, pT0 + kt * 512:pT0 + (kt + 1) * 512]
            ctxb = [arena[:, 40960 + i * 4096:40960 + (i + 1) * 4096]
                    for i in range(2)]
            head_end = {}
            cx_st_war = {}
            zc_last_ev = {0: 0, 1: 0}
            for h in range(NH):
                p = h % 2
                qh, kh, vh = AH[p], AH[2 + p], AH[4 + p]
                vhv = vh.rearrange("p (kc d) -> p kc d", kc=32)
                ld = d_hd[p]
                ws = [(c_t, head_end.get(h - 2, p5_end_t if h < 2 else 0))]
                if h == 0:
                    ws += qk_done + v_done + [(c_g, masks_done_g)]
                op("sync", lambda e, h=h, qh=qh: e.dma_start(
                    out=qh, in_=g_qT[h * 128:(h + 1) * 128, :]),
                   waits=ws, inc=ld, inc_n=16)
                op("sync", lambda e, h=h, kh=kh: e.dma_start(
                    out=kh, in_=g_kT[h * 128:(h + 1) * 128, :]),
                   inc=ld, inc_n=16)
                op("sync", lambda e, h=h, vhv=vhv: e.dma_start(
                    out=vhv,
                    in_=g_v.rearrange("(kc p) d -> p kc d", p=128)[
                        :, :, h * 128:(h + 1) * 128]),
                   inc=ld, inc_n=16)
                loads_rdy = ld.n
                if h == 0 and "g_dbg_q" in DEBUG_TAPS:
                    op("scalar", lambda e, qh=qh: e.dma_start(
                        out=g_dbg_q, in_=qh), waits=[(ld, loads_rdy)],
                       inc=w_wb, inc_n=16)
                    op("scalar", lambda e, kh=kh: e.dma_start(
                        out=g_dbg_k, in_=kh), waits=[(ld, loads_rdy)],
                       inc=w_wb, inc_n=16)
                ctx_end_prev = 0
                for qc in range(P6_NQC):
                    KT = 4 * (qc + 1)
                    if qc > 0:
                        wait("scalar", (c_t, ctx_end_prev))
                    psZ = psb[4 + (qc % 2)]
                    psC = psb[6 + (qc % 2)]
                    zc_war = zc_last_ev[qc % 2]
                    exp_done = {}
                    p_rdy = {}

                    def emit_zc(kt, first, last, psZ=psZ, psC=psC, vhv=vhv,
                                p_rdy=p_rdy, qc=qc):
                        off = max(0, kt - 4 * qc) * 128
                        op("tensor",
                           lambda e, kt=kt, first=first, last=last, off=off:
                           e.matmul(psZ[:, off:512], ones_bf[:],
                                    pTs(kt)[:, off:512],
                                    start=first, stop=last,
                                    skip_group_check=True),
                           waits=[p_rdy[kt]] +
                                 ([(c_v, zc_war)] if first else []))
                        op("tensor",
                           lambda e, kt=kt, first=first, last=last, off=off:
                           e.matmul(psC[:, off:512], vhv[:, kt, :],
                                    pTs(kt)[:, off:512],
                                    start=first, stop=last,
                                    skip_group_check=True),
                           inc=(c_t if last else None))

                    for kt in range(KT):
                        bank = psb[kt % 2]
                        mw = [(c_s, exp_done.get(kt - 2, 0))]
                        if P6_SERIAL:
                            mw = [(c_s, c_s.n), (c_v, c_v.n)]
                        if kt == 0 and qc == 0:
                            mw.append((ld, loads_rdy))
                        soff = max(0, kt - 4 * qc) * 128
                        op("tensor",
                           lambda e, bank=bank, kt=kt, qc=qc, kh=kh, qh=qh,
                           soff=soff:
                           e.matmul(bank[:, soff:512],
                                    kh[:, kt * 128:(kt + 1) * 128],
                                    qh[:, qc * 512 + soff:(qc + 1) * 512],
                                    start=True, stop=True,
                                    skip_group_check=True),
                           waits=mw, inc=c_t)
                        t_sc = c_t.n
                        op("scalar",
                           lambda e, bank=bank, kt=kt, soff=soff:
                           e.activation(
                               pTs(kt)[:, soff:512], bank[:, soff:512],
                               AF.Exp,
                               bias=f32(EXP_BIAS), scale=f32(EXP_SCALE)),
                           waits=[(c_t, t_sc)], inc=c_s)
                        exp_done[kt] = c_s.n
                        diag = kt - 4 * qc
                        if diag >= 0:
                            op("vector", lambda e, kt=kt, diag=diag:
                               e.tensor_tensor(
                                   pTs(kt), pTs(kt),
                                   masks[:, diag * 512:(diag + 1) * 512],
                                   ALU.mult),
                               waits=[(c_s, exp_done[kt])] +
                                     ([(c_g, masks_done_g)]
                                      if (h == 0 and qc == 0) else []),
                               inc=c_v)
                            p_rdy[kt] = (c_v, c_v.n)
                        else:
                            p_rdy[kt] = (c_s, exp_done[kt])
                        if kt >= 2:
                            emit_zc(kt - 2, (kt - 2) == 0, False)
                    emit_zc(KT - 2, KT == 2, False)
                    emit_zc(KT - 1, False, True)
                    ctx_end_prev = c_t.n
                    vw = [(c_t, ctx_end_prev)]
                    if qc == 0:
                        vw.append((w_cx[p], cx_st_war.get(("st", h - 2), 0)))
                    op("vector", lambda e, psZ=psZ: e.reciprocal(
                        rtmp[:, 0:512], psZ[:]), waits=vw)
                    op("vector", lambda e, psC=psC, cb=ctxb[p], qc=qc:
                       e.tensor_tensor(cb[:, qc * 512:(qc + 1) * 512],
                                       psC[:], rtmp[:, 0:512], ALU.mult),
                       inc=c_v)
                    cx_st_war[("ev", h, qc)] = c_v.n
                    zc_last_ev[qc % 2] = c_v.n
                    if h == 0 and qc == 0 and "g_dbg_p" in DEBUG_TAPS:
                        dv = c_v.n
                        op("scalar", lambda e: e.dma_start(
                            out=g_dbg_p, in_=arena[:, pT0:pT0 + 2048]),
                           waits=[(c_v, dv)], inc=w_wb, inc_n=16)
                        op("scalar", lambda e: e.dma_start(
                            out=g_dbg_z, in_=rtmp[:, 0:512]),
                           waits=[(c_v, dv)], inc=w_wb, inc_n=16)
                head_end[h] = c_t.n
                v_cx = c_v.n
                op("scalar", lambda e, h=h, cb=ctxb[p]: e.dma_start(
                    out=g_ctxT[h * 128:(h + 1) * 128, :], in_=cb),
                   waits=[(c_v, v_cx)], inc=w_cx[p], inc_n=16)
                cx_st_war[("st", h)] = w_cx[p].n
            p6_end_t = c_t.n
            cx_done = [(w_cx[0], w_cx[0].n), (w_cx[1], w_cx[1].n)]

        d_cx = spair("dcx")
        p7_end_t = p6_end_t
        ln3_g = {}
        if "p7" in phases:
            wo_sb = arena[:, 0:HC * 2048].rearrange("p (dc o) -> p dc o",
                                                    dc=HC)
            s_now = c_s.n
            for q4 in range(4):
                op("sync", lambda e, q4=q4: e.dma_start(
                    out=wo_sb[:, q4 * 4:(q4 + 1) * 4, :],
                    in_=g_wo.rearrange("(hc p) d -> p hc d", p=128)[
                        :, q4 * 4:(q4 + 1) * 4, :]),
                   waits=([(c_t, p6_end_t), (c_s, s_now)]
                          if q4 == 0 else []),
                   inc=d_wv, inc_n=16)
            wo_ready = d_wv.n
            st_ccol, st_evx, st_evk = {}, {}, {}
            p3_y = {w_y[0]: w_y[0].n, w_y[1]: w_y[1].n}
            for st in range(ST):
                csem = d_cx[st % 2]
                cc = qkbuf[st % 2]
                ccv = cc.rearrange("p (nh s) -> p nh s", nh=NH)
                op("sync", lambda e, st=st, ccv=ccv: e.dma_start(
                    out=ccv,
                    in_=g_ctxT.rearrange("(nh p) s -> p nh s", p=128)[
                        :, :, st * 128:(st + 1) * 128]),
                   waits=(cx_done if st == 0 else []) +
                         [(c_t, st_ccol.get(st - 2, 0))],
                   inc=csem, inc_n=16)
                cc_ready = csem.n
                ld = d_yr[st % 2]
                xb = xtile[st % 2]
                op("sync", lambda e, st=st, xb=xb: e.dma_start(
                    out=xb[:], in_=g_y1[st * 128:(st + 1) * 128, :]),
                   waits=[(c_v, st_evx.get(st - 2, 0))],
                   inc=ld, inc_n=16)
                x_ready = ld.n
                yb = ybuf[st % 2]
                wsy = w_y[st % 2]
                for oc4 in range(4):
                    kk = st * 4 + oc4
                    ps = psb[4 + kk % 2]

                    def mm(e, ccv=ccv, ps=ps, oc4=oc4):
                        r = None
                        for dc in range(HC):
                            r = e.matmul(ps[:], ccv[:, dc, :],
                                         wo_sb[:, dc,
                                               oc4 * 512:(oc4 + 1) * 512],
                                         start=(dc == 0), stop=(dc == HC - 1))
                        return r
                    op("tensor", mm,
                       waits=[(d_wv, wo_ready), (csem, cc_ready),
                              (c_v, st_evk.get(kk - 2, 0))],
                       inc=c_t)
                    t_mm = c_t.n
                    st_ccol[st] = t_mm
                    vw = [(c_t, t_mm), (ld, x_ready)]
                    if oc4 == 0:
                        vw.append((wsy, wsy.n if st >= 2 else p3_y[wsy]))
                        if st >= 2:
                            vw.append(ln3_g[st - 2])
                    op("vector", lambda e, ps=ps, xb=xb, yb=yb, oc4=oc4:
                       e.tensor_tensor(yb[:, oc4 * 512:(oc4 + 1) * 512],
                                       ps[:],
                                       xb[:, oc4 * 512:(oc4 + 1) * 512],
                                       ALU.add),
                       waits=vw, inc=c_v)
                    st_evk[kk] = c_v.n
                st_evx[st] = c_v.n
                v_ev = c_v.n
                op("scalar", lambda e, st=st, yb=yb: e.dma_start(
                    out=g_y2[st * 128:(st + 1) * 128, :], in_=yb[:]),
                   waits=[(c_v, v_ev)], inc=wsy, inc_n=16)
                ln3_g[st] = emit_ln_tile(yb[:], [(c_v, v_ev)],
                                         eng="vector")
            y2_done = [(w_y[0], w_y[0].n), (w_y[1], w_y[1].n)]
            p7_end_t = c_t.n

        if "p8" in phases:
            p8a_end = emit_gate_up("f2", [], [(c_t, p7_end_t)])
            emit_down("f2", g_y2, y_out, w_out, y2_done, p8a_end,
                      ({0: ln3_g[ST - 2], 1: ln3_g[ST - 1]}
                       if ln3_g else {}))
            wait("sync", (w_out[0], w_out[0].n), (w_out[1], w_out[1].n))

        drains = [w_xln, w_h, w_y, w_qk, w_v, w_cx, w_out, d_x]
        for sp in drains:
            wait("sync", (sp[0], sp[0].n), (sp[1], sp[1].n))
        for s1 in (w_wm, w_wb, w_cst, d_lut):
            wait("sync", (s1, s1.n))

        blk = ctx.enter_context(nc.Block())
        for eng in ENGS:
            def runner(e, eng=eng):
                for fn in streams[eng]:
                    fn(e)
            getattr(blk, eng)(runner)

    return nc


def kernel(**inputs):
    x = np.ascontiguousarray(np.asarray(inputs["L0_attn_hidden_states"],
                                        dtype=np.float32))
    wqT = np.ascontiguousarray(np.asarray(inputs["L0_attn_Wq"],
                                          dtype=np.float32).T)
    wkT = np.ascontiguousarray(np.asarray(inputs["L0_attn_Wk"],
                                          dtype=np.float32).T)
    wvT = np.ascontiguousarray(np.asarray(inputs["L0_attn_Wv"],
                                          dtype=np.float32).T)
    woT = np.ascontiguousarray(np.asarray(inputs["L0_attn_Wo"],
                                          dtype=np.float32).T)
    lut = np.ascontiguousarray(np.asarray(inputs["L0_ffn_lut"],
                                          dtype=np.float32).reshape(1, 16))
    gidxT = np.asarray(inputs["L0_ffn_gate_idx"], dtype=np.int32).T
    uidxT = np.asarray(inputs["L0_ffn_up_idx"], dtype=np.int32).T
    didxT = np.asarray(inputs["L0_ffn_down_idx"], dtype=np.int32).T
    dpad = np.zeros((6144, H), np.int32)
    dpad[0:O] = didxT

    nc = build_program()
    in_maps = [{
        "x": np.ascontiguousarray(x[b]),
        "wqs": np.ascontiguousarray(wqT[b * 256:(b + 1) * 256]),
        "wks": np.ascontiguousarray(wkT[b * 256:(b + 1) * 256]),
        "wvs": np.ascontiguousarray(wvT[b * 256:(b + 1) * 256]),
        "wos": np.ascontiguousarray(woT[b * 256:(b + 1) * 256]),
        "lut": lut,
        "guidxT": np.ascontiguousarray(
            gidxT[b * 512:(b + 1) * 512] if b < 4
            else uidxT[(b - 4) * 512:(b - 3) * 512]),
        "didxTs": np.ascontiguousarray(dpad[b * 768:(b + 1) * 768]),
    } for b in range(8)]
    res = run_bass_kernel_spmd(nc, in_maps, core_ids=list(range(8)))
    return np.stack([res.results[b]["y"] for b in range(8)], axis=0)
